# version 10
# baseline (speedup 1.0000x reference)
import contextlib
import numpy as np
import ml_dtypes
import concourse.bass as bass
import concourse.mybir as mybir
from concourse.bass_utils import run_bass_kernel_spmd
from concourse.bass_isa import ReduceOp

F32 = mybir.dt.float32
BF16 = mybir.dt.bfloat16
I32 = mybir.dt.int32
Alu = mybir.AluOpType
Act = mybir.ActivationFunctionType
AX = mybir.AxisListType.X

DM = 2048
NH = 16
CH = 128
KT = DM // 128
SCALE_O = float(128) ** -0.5


class SemT:
    def __init__(self, bld, name):
        self.name = name
        self.h = bld.ctx.enter_context(bld.nc.semaphore(name))
        self.v = 0

    def mark(self):
        return (self, self.v)


class Eng:
    def __init__(self, name, h):
        self.name = name
        self.h = h


class Bld:
    def __init__(self, nc, ctx):
        self.nc = nc
        self.ctx = ctx
        self.sems = {}
        self.log = {e: [] for e in ("sync", "scalar", "vector", "tensor", "gpsimd")}
        self.bank_guard = {}

    def sem(self, name):
        if name not in self.sems:
            self.sems[name] = SemT(self, name)
        return self.sems[name]

    def w(self, eng, mark):
        if mark is None:
            return
        if isinstance(mark, tuple) and len(mark) == 2 and isinstance(mark[0], tuple):
            self.w(eng, mark[0]); self.w(eng, mark[1]); return
        sem, val = mark
        if val <= 0:
            return
        eng.h.wait_ge(sem.h, val)
        self.log[eng.name].append(("wait", sem.name, val))

    def i(self, eng, instr, semname, n=1):
        sem = self.sem(semname)
        instr.then_inc(sem.h, n)
        sem.v += n
        self.log[eng.name].append(("inc", sem.name, n))
        return sem.mark()

    def f(self, eng, instr, semname):
        mk = self.i(eng, instr, semname)
        self.w(eng, mk)
        return mk

    def guard_bank(self, te, idx):
        self.w(te, self.bank_guard.get(idx))

    def barrier(self, engs):
        marks = [(s, s.v) for s in list(self.sems.values())]
        for e in engs:
            for mk in marks:
                self.w(e, mk)

    def audit(self):
        ptr = {e: 0 for e in self.log}
        vals = {}
        progress = True
        while progress:
            progress = False
            for e, evs in self.log.items():
                while ptr[e] < len(evs):
                    kind, s, n = evs[ptr[e]]
                    if kind == "wait" and vals.get(s, 0) < n:
                        break
                    if kind == "inc":
                        vals[s] = vals.get(s, 0) + n
                    ptr[e] += 1
                    progress = True
        stuck = {e: (ptr[e], len(evs), evs[ptr[e]]) for e, evs in self.log.items()
                 if ptr[e] < len(evs)}
        if stuck:
            raise RuntimeError(f"deadlock audit failed: {stuck}")


def build_program(nch=16, stop_after="P4", dbg_scratch=False):
    NT = nch * CH
    GSZ = 512 if NT >= 512 else NT
    NG = NT // GSZ
    nc = bass.Bass("TRN2", target_bir_lowering=False)
    ctx = contextlib.ExitStack()
    B = Bld(nc, ctx)
    sy = Eng("sync", nc.sync)
    sc = Eng("scalar", nc.scalar)
    ve = Eng("vector", nc.vector)
    te = Eng("tensor", nc.tensor)
    gp = Eng("gpsimd", nc.gpsimd)
    SY, SC, VE, TE, GP = nc.sync, nc.scalar, nc.vector, nc.tensor, nc.gpsimd

    x_d = nc.dram_tensor("x", [NT, DM], F32, kind="ExternalInput").ap()
    qkvw_d = nc.dram_tensor("qkv_w", [3 * DM, DM], F32, kind="ExternalInput").ap()
    projw_d = nc.dram_tensor("proj_w", [DM, DM], F32, kind="ExternalInput").ap()
    ffnw_d = nc.dram_tensor("ffn_w", [DM, DM], F32, kind="ExternalInput").ap()
    qn_d = nc.dram_tensor("qkv_norm_w", [128, DM], F32, kind="ExternalInput").ap()
    pn_d = nc.dram_tensor("proj_norm_w", [128, DM], F32, kind="ExternalInput").ap()
    nw_d = nc.dram_tensor("norm_w", [128, DM], F32, kind="ExternalInput").ap()
    s_d = nc.dram_tensor("s_dec", [128, NH], F32, kind="ExternalInput").ap()
    ident_d = nc.dram_tensor("c_ident", [128, 128], BF16, kind="ExternalInput").ap()
    tt_d = nc.dram_tensor("c_tt", [128, 128], F32, kind="ExternalInput").ap()
    diffc_d = nc.dram_tensor("c_diffc", [128, 128], F32, kind="ExternalInput").ap()
    lmask_d = nc.dram_tensor("c_lmask", [128, 128], F32, kind="ExternalInput").ap()
    tmin_d = nc.dram_tensor("c_tmin", [128, 1], F32, kind="ExternalInput").ap()
    eps_d = nc.dram_tensor("c_eps", [128, 8], F32, kind="ExternalInput").ap()
    y_d = nc.dram_tensor("y", [NT, DM], F32, kind="ExternalOutput").ap()
    skind = dict(kind="ExternalOutput") if dbg_scratch else {}
    g_s = nc.dram_tensor("g_s", [nch, 128, DM], F32, **skind).ap()
    qk_s = nc.dram_tensor("qk_s", [2 * NH, 128, NT], BF16, **skind).ap()
    v_s = nc.dram_tensor("v_s", [4, nch, 128, 512], BF16, **skind).ap()
    xq_s = nc.dram_tensor("xq_s", [nch, 128, DM], BF16, **skind).ap()
    h_sdbg = nc.dram_tensor("h_sdbg", [nch, 128, DM], BF16, kind="ExternalOutput").ap() if dbg_scratch else None
    ao_dbg = nc.dram_tensor("ao_dbg", [128, DM], F32, kind="ExternalOutput").ap() if dbg_scratch else None
    cw_dbg = nc.dram_tensor("cw_dbg", [128, 64], F32, kind="ExternalOutput").ap() if dbg_scratch else None
    r2_dbg = nc.dram_tensor("r2_dbg", [128, DM], BF16, kind="ExternalOutput").ap() if dbg_scratch else None
    sg_dbg = nc.dram_tensor("sg_dbg", [128, DM], F32, kind="ExternalOutput").ap() if dbg_scratch else None
    xn_dbg = nc.dram_tensor("xn_dbg", [128, DM], F32, kind="ExternalOutput").ap() if dbg_scratch else None
    z_dbg = nc.dram_tensor("z_dbg", [128, DM], F32, kind="ExternalOutput").ap() if dbg_scratch else None
    st_dbg = nc.dram_tensor("st_dbg", [128, 16], F32, kind="ExternalOutput").ap() if dbg_scratch else None

    sb = lambda st, name, shp, dt: st.enter_context(nc.sbuf_tensor(name, shp, dt))

    ident = sb(ctx, "ident", [128, 128], BF16)
    cons = sb(ctx, "cons", [128, 384], F32)
    tmin = sb(ctx, "tmin", [128, 1], F32)
    eps = sb(ctx, "eps", [128, 8], F32)
    svec = sb(ctx, "svec", [128, NH * 4], F32)
    colw = sb(ctx, "colw", [128, 64], F32)
    stats = sb(ctx, "stats", [128, 16], F32)
    pb = [ctx.enter_context(nc.psum_tensor(f"pb{i}", [128, 512], F32)) for i in range(8)]
    ones = sb(ctx, "ones", [128, 128], F32)
    hfmst = contextlib.ExitStack()
    ctx.enter_context(hfmst)
    hfm = sb(hfmst, "hfm", [128, KT * NT], BF16)

    EPS_LN = eps[:, 0:1]
    EPS_M = eps[:, 1:2]
    EPS_8 = eps[:, 2:3]
    EPS_6 = eps[:, 3:4]
    THRQ, MAGQ = colw[:, 16:17], colw[:, 17:18]
    NTHRQ, NTHRP, NTHRF = colw[:, 27:28], colw[:, 28:29], colw[:, 29:30]
    THRP, MAGP = colw[:, 18:19], colw[:, 19:20]
    THRF, WSF = colw[:, 20:21], colw[:, 21:22]
    ACC_Q = colw[:, 22:23]
    ACC_F = colw[:, 23:24]
    ACC_P = colw[:, 24:25]
    TMP0, TMP1 = colw[:, 25:26], colw[:, 26:27]

    TRASH0, TRASH1 = stats[:, 14:15], stats[:, 15:16]

    def vpub():
        return B.f(ve, VE.tensor_copy(TRASH0, TRASH1), "pubv")

    def spub():
        return B.f(sc, SC.copy(colw[:, 56:57], colw[:, 57:58]), "pubs")

    st_sum, st_m, st_r, st_b, st_v = (stats[:, i:i + 1] for i in range(5))
    st_a, st_q, st_r2, st_rr, st_brr = (stats[:, i:i + 1] for i in range(5, 10))
    st_f, st_t0, st_t1, st_ssq = (stats[:, i:i + 1] for i in range(10, 14))

    B.i(sy, SY.dma_start(out=ident[:], in_=ident_d[:]), "ld0", 16)
    B.i(sy, SY.dma_start(out=cons[:, 0:128], in_=tt_d[:]), "ld0", 16)
    B.i(sy, SY.dma_start(out=cons[:, 128:256], in_=diffc_d[:]), "ld0", 16)
    B.i(sy, SY.dma_start(out=cons[:, 256:384], in_=lmask_d[:]), "ld0", 16)
    B.i(sy, SY.dma_start(out=tmin[:], in_=tmin_d[:]), "ld0", 16)
    B.i(sy, SY.dma_start(out=eps[:], in_=eps_d[:]), "ld0", 16)
    mk_ld0 = B.i(sy, SY.dma_start(out=svec[:, 0:NH], in_=s_d[:]), "ld0", 16)

    B.i(ve, VE.memset(stats[:, 14:16], 0.0), "p0v")
    B.i(ve, VE.memset(colw[:, 56:58], 0.0), "p0v")
    B.f(ve, VE.memset(colw[:, 48:49], float(3 * DM * DM)), "p0v")
    B.f(ve, VE.reciprocal(colw[:, 49:50], colw[:, 48:49]), "p0v")
    B.f(ve, VE.memset(colw[:, 50:51], 127.0), "p0v")
    B.f(ve, VE.reciprocal(colw[:, 51:52], colw[:, 50:51]), "p0v")
    RC3M = colw[:, 49:50]
    R127 = colw[:, 51:52]
    OSC = colw[:, 53:54]
    EPSM_AD = colw[:, 55:56]
    B.f(ve, VE.memset(colw[:, 52:53], 128.0), "p0v")
    B.f(ve, VE.memset(colw[:, 54:55], 3.0), "p0v")
    B.f(ve, VE.reciprocal(EPSM_AD, colw[:, 54:55]), "p0v")
    B.f(ve, VE.tensor_scalar(EPSM_AD, EPSM_AD, 4.0, -1.0, Alu.mult, Alu.add), "p0v")
    B.f(ve, VE.tensor_scalar(EPSM_AD, EPSM_AD, 3.0, -1.0, Alu.mult, Alu.add), "p0v")
    B.f(ve, VE.tensor_scalar(colw[:, 54:55], EPSM_AD, -1.0, None, Alu.mult), "p0v")
    B.f(ve, VE.tensor_tensor(EPSM_AD, EPSM_AD, colw[:, 54:55], Alu.max), "p0v")
    B.w(ve, mk_ld0)
    mk_ones = B.i(ve, VE.memset(ones[:], 1.0), "p0v")
    B.f(ve, VE.tensor_scalar(svec[:, NH:2 * NH], svec[:, 0:NH], -1.0, None, Alu.mult), "p0v")
    B.f(ve, VE.tensor_scalar(svec[:, 2 * NH:3 * NH], svec[:, NH:2 * NH], tmin[:, 0:1], None, Alu.mult), "p0v")
    mk_lr0 = vpub()
    B.w(sc, mk_lr0)
    B.f(sc, SC.activation(svec[:, 2 * NH:3 * NH], svec[:, 2 * NH:3 * NH], Act.Exp), "p0s")
    B.f(sc, SC.activation(svec[:, 3 * NH:4 * NH], svec[:, 0:NH], Act.Exp, scale=-128.0), "p0s")
    B.f(sc, SC.activation(OSC, colw[:, 52:53], Act.Ln), "p0s")
    B.f(sc, SC.activation(OSC, OSC, Act.Exp, scale=-0.5), "p0s")
    mk_lamT = spub()
    LAMREV = lambda h: svec[:, 2 * NH + h:2 * NH + h + 1]
    LAMT = lambda h: svec[:, 3 * NH + h:3 * NH + h + 1]

    if stop_after == "P0":
        B.w(ve, mk_lamT)
        B.barrier([gp])
        B.audit()
        return nc, ctx, None

    st2 = contextlib.ExitStack()
    ctx.enter_context(st2)
    xb = [sb(st2, f"xb{i}", [128, DM], F32) for i in range(2)]
    dbufs = [sb(st2, f"dbuf{i}", [128, DM], F32) for i in range(2)]
    zbufs = [sb(st2, f"zbuf{i}", [128, DM], F32) for i in range(2)]
    uibs = [sb(st2, f"uib{i}", [128, DM], I32) for i in range(2)]
    dmp = sb(st2, "dmp", [128, DM], BF16)
    cc1 = sb(st2, "cc1", [128, DM], BF16)
    cc2 = sb(st2, "cc2", [128, DM], BF16)
    xqtm = [sb(st2, f"xqtm{i}", [128, DM], BF16) for i in range(2)]
    htm = [sb(st2, f"htm{i}", [128, DM], BF16) for i in range(2)]
    xqfm = [sb(st2, f"xqfm{i}", [128, DM], BF16) for i in range(2)]
    qwb = [sb(st2, f"qwb{i}", [128, DM], F32) for i in range(2)]
    qnb = sb(st2, "qnb", [128, DM], F32)
    p2stats = sb(st2, "p2stats", [128, 24], F32)

    mk_qnb = B.i(sy, SY.dma_start(out=qnb[:], in_=qn_d[:]), "qnbld", 16)

    def part_sum(acc_ap, out_ap, wait_mark):
        B.w(te, wait_mark)
        B.w(te, mk_ones)
        B.guard_bank(te, 6)
        mk = B.i(te, TE.matmul(pb[6][:, 0:1], ones[:], acc_ap, start=True, stop=True), "psum_mm")
        B.w(sc, mk)
        mk_ev = B.f(sc, SC.copy(out_ap, pb[6][:, 0:1]), "psum_ev")
        B.bank_guard[6] = mk_ev
        return spub()

    NWQ = 3 * KT
    mk_qwld = [None] * NWQ
    mk_qwabs = [None] * NWQ
    qw_next = [0]

    def pump_qw(kmax):
        while qw_next[0] < min(kmax, NWQ):
            j = qw_next[0]
            if j > 1:
                B.w(gp, mk_qwabs[j - 2])
            mk_qwld[j] = B.i(gp, GP.dma_start(out=qwb[j % 2][:], in_=qkvw_d[128 * j:128 * (j + 1), :]), f"qwld{j % 2}", 16)
            B.w(ve, mk_qwld[j])
            B.f(ve, VE.reduce_sum(TMP0, qwb[j % 2][:], axis=AX, apply_absolute_value=True), "qwabs")
            if j == 0:
                mk_qwabs[j] = B.f(ve, VE.tensor_copy(ACC_Q, TMP0), "qwabs")
            else:
                mk_qwabs[j] = B.f(ve, VE.tensor_tensor(ACC_Q, ACC_Q, TMP0, Alu.add), "qwabs")
            qw_next[0] += 1

    mk_xld = [None] * nch
    mk_xfree = [None] * nch
    mk_hfm = [None] * nch
    mk_xqs = {0: None, 1: None}
    mk_xq_used = {0: None, 1: None}
    mk_htm_used = {0: None, 1: None}
    pend_tp = []
    for c in range(nch):
        p = c % 2
        xbp, dbp, zbp, uibp = xb[p], dbufs[p], zbufs[p], uibs[p]
        SS = lambda i: p2stats[:, 12 * p + i:12 * p + i + 1]
        SA, SB, M, NB, T, V, U, RR, AQ, QP, RL = (SS(i) for i in range(11))
        if c >= 2:
            B.w(sy, mk_xfree[c - 2])
        mk_xld[c] = B.i(sy, SY.dma_start(out=xbp[:], in_=x_d[CH * c:CH * (c + 1), :]), f"xld{p}", 16)
        pump_qw(3 * (c + 1))

        B.w(sc, mk_xld[c])
        B.f(sc, SC.activation(dmp[:], xbp[:], Act.Copy, accum_out=SA), "p2s")
        B.f(sc, SC.activation(dmp[:], xbp[:], Act.Square, accum_out=SB), "p2s")
        mk_acc = spub()
        if pend_tp:
            pend_tp.pop(0)()

        B.w(ve, mk_acc)
        B.f(ve, VE.tensor_scalar(M, SA, 1.0 / DM, None, Alu.mult), "p2v")
        B.f(ve, VE.tensor_scalar(NB, M, -1.0, None, Alu.mult), "p2v")
        B.f(ve, VE.tensor_scalar(T, SB, 1.0 / DM, None, Alu.mult), "p2v")
        B.f(ve, VE.tensor_tensor(U, M, M, Alu.mult), "p2v")
        B.f(ve, VE.tensor_tensor(V, T, U, Alu.subtract), "p2v")
        B.f(ve, VE.tensor_tensor(U, V, EPS_LN, Alu.add), "p2v")
        B.f(ve, VE.tensor_scalar(T, U, EPSM_AD, None, Alu.mult), "p2v")
        B.f(ve, VE.tensor_tensor(T, V, T, Alu.add), "p2v")
        mk_arg = vpub()
        B.w(sc, mk_arg)
        B.f(sc, SC.activation(RL, T, Act.Ln), "p2s")
        B.f(sc, SC.activation(RR, RL, Act.Exp, scale=-0.5), "p2s")
        B.f(sc, SC.activation(RL, U, Act.Ln), "p2s")
        B.f(sc, SC.activation(RL, RL, Act.Exp, scale=-0.5), "p2s")
        mk_rr = spub()

        B.w(ve, mk_rr)
        mk_d = B.f(ve, VE.tensor_scalar(dbp[:], xbp[:], NB, None, Alu.add), "p2v")
        B.f(ve, VE.reduce_max(AQ, dbp[:], axis=AX, apply_absolute_value=True), "p2v")
        B.f(ve, VE.tensor_tensor(AQ, AQ, EPS_8, Alu.max), "p2v")
        B.f(ve, VE.tensor_scalar(colw[:, 32 + c:33 + c], AQ, RL, None, Alu.mult), "p2v")
        B.f(ve, VE.reciprocal(T, AQ), "p2v")
        B.f(ve, VE.tensor_scalar(QP, T, 127.0, None, Alu.mult), "p2v")
        B.w(ve, mk_qnb)
        B.w(ve, mk_htm_used[p])
        B.f(ve, VE.scalar_tensor_tensor(htm[p][:], dbp[:], RR, qnb[:], Alu.mult, Alu.mult), "p2v")
        mk_htm = vpub()
        B.w(sc, mk_htm)
        mk_z = B.f(sc, SC.activation(zbp[:], dbp[:], Act.Copy, scale=QP), "p2s")
        mk_zp = spub()

        B.w(ve, mk_zp)
        B.f(ve, VE.tensor_copy(uibp[:], zbp[:]), "p2v")
        B.f(ve, VE.tensor_copy(xbp[:], uibp[:]), "p2v")
        B.f(ve, VE.scalar_tensor_tensor(cc1[:], zbp[:], 0.5, xbp[:], Alu.subtract, Alu.is_gt), "p2v")
        B.f(ve, VE.scalar_tensor_tensor(cc2[:], zbp[:], 0.5, xbp[:], Alu.add, Alu.is_lt), "p2v")
        mk_cc = vpub()
        B.w(gp, mk_cc)
        B.w(gp, mk_xq_used[p])
        B.f(gp, GP.tensor_tensor(xqtm[p][:], xbp[:], cc1[:], Alu.add), "p2g")
        mk_xq = B.f(gp, GP.tensor_tensor(xqtm[p][:], xqtm[p][:], cc2[:], Alu.subtract), "p2g")
        mk_xfree[c] = mk_xq

        def do_transposes(c=c, p=p, mk_xq=mk_xq, mk_htm=mk_htm):
            B.w(te, mk_xq)
            if c == 0:
                B.w(te, mk_ld0)
            B.w(sc, mk_xqs[p])
            for k4 in range(0, KT, 4):
                bidx = (k4 // 4) % 4
                bank = pb[bidx]
                B.guard_bank(te, bidx)
                for kk in range(4):
                    k = k4 + kk
                    B.i(te, TE.matmul(bank[:, 128 * kk:128 * (kk + 1)], xqtm[p][:, 128 * k:128 * (k + 1)], ident[:], start=True, stop=True), "p2tp")
                mk_tp = B.sems["p2tp"].mark()
                B.w(sc, mk_tp)
                mk_ev = B.i(sc, SC.copy(xqfm[p][:].rearrange("p (k t) -> p k t", t=128)[:, k4:k4 + 4, :], bank[:].rearrange("p (k t) -> p k t", t=128)), "p2ev")
                B.bank_guard[bidx] = mk_ev
            mk_xq_used[p] = B.sems["p2tp"].mark()
            mk_xqfm = B.sems["p2ev"].mark()
            B.w(gp, mk_xqfm)
            mk_xqs[p] = B.i(gp, GP.dma_start(out=xq_s[c, :, :], in_=xqfm[p][:]), f"xqs{p}", 16)

            B.w(te, mk_htm)
            for k4 in range(0, KT, 4):
                bidx = 4 + (k4 // 4) % 4
                bank = pb[bidx]
                B.guard_bank(te, bidx)
                for kk in range(4):
                    k = k4 + kk
                    B.i(te, TE.matmul(bank[:, 128 * kk:128 * (kk + 1)], htm[p][:, 128 * k:128 * (k + 1)], ident[:], start=True, stop=True), "p2tp")
                mk_tp = B.sems["p2tp"].mark()
                B.w(sc, mk_tp)
                dst = hfm[:].rearrange("p (k t) -> p k t", t=NT)[:, k4:k4 + 4, CH * c:CH * (c + 1)]
                mk_ev = B.i(sc, SC.copy(dst, bank[:].rearrange("p (k t) -> p k t", t=128)), "p2ev")
                B.bank_guard[bidx] = mk_ev
            mk_htm_used[p] = B.sems["p2tp"].mark()
            mk_hfm[c] = B.sems["p2ev"].mark()
            if h_sdbg is not None:
                B.w(gp, mk_htm)
                mk_hdbg = B.i(gp, GP.dma_start(out=h_sdbg[c, :, :], in_=htm[p][:]), "dbgs", 16)
                mk_htm_used[p] = (mk_htm_used[p], mk_hdbg)

        pend_tp.append(do_transposes)

    while pend_tp:
        pend_tp.pop(0)()
    pump_qw(NWQ)
    mk_qar = part_sum(ACC_Q, TMP1, mk_qwabs[NWQ - 1])
    B.w(ve, mk_qar)
    B.f(ve, VE.tensor_scalar(TMP1, TMP1, RC3M, None, Alu.mult), "qkq")
    B.f(ve, VE.tensor_scalar(TMP1, TMP1, 1e-5, None, Alu.max), "qkq")
    B.f(ve, VE.tensor_scalar(THRQ, TMP1, 0.5, None, Alu.mult), "qkq")
    B.f(ve, VE.tensor_scalar(NTHRQ, TMP1, -0.5, None, Alu.mult), "qkq")
    B.f(ve, VE.reciprocal(TMP0, TMP1), "qkq")
    B.f(ve, VE.reciprocal(MAGQ, TMP0), "qkq")
    mk_qthr = vpub()

    if stop_after == "P2a":
        B.barrier([gp])
        B.audit()
        return nc, ctx, None
    B.barrier([sy, sc, ve, te, gp])
    st2.close()

    st3 = contextlib.ExitStack()
    ctx.enter_context(st3)
    wbuf = [sb(st3, f"wbuf{i}", [128, DM], F32) for i in range(2)]
    qc1 = sb(st3, "qc1", [128, DM], BF16)
    qc2 = sb(st3, "qc2", [128, DM], BF16)
    tern = [sb(st3, f"tern{i}", [128, DM], BF16) for i in range(2)]
    wqt = [sb(st3, f"wqt{i}", [128, DM], BF16) for i in range(2)]
    wvt = sb(st3, "wvt", [128, KT * 512], BF16)
    qev = [sb(st3, f"qev{i}", [128, 512], BF16) for i in range(2)]
    fwb = sb(st3, "fwb", [128, DM], F32)

    mk_tern = {}
    tern_free = {0: None, 1: None}
    wbuf_free = {0: None, 1: None}

    def quant_stream(j, src, thr_ap, nthr_ap, extra_v_wait=None):
        pbuf = j % 2
        B.w(sy, wbuf_free[pbuf])
        mk_ld = B.i(sy, SY.dma_start(out=wbuf[pbuf][:], in_=src), f"p3wld{pbuf}", 16)
        B.w(ve, mk_ld)
        if extra_v_wait is not None:
            B.w(ve, extra_v_wait)
        B.w(ve, tern_free[pbuf])
        B.f(ve, VE.tensor_scalar(qc1[:], wbuf[pbuf][:], thr_ap, None, Alu.is_gt), "p3q")
        mk_c2 = B.f(ve, VE.tensor_scalar(qc2[:], wbuf[pbuf][:], nthr_ap, None, Alu.is_lt), "p3q")
        wbuf_free[pbuf] = mk_c2
        mk_tern[j] = B.f(ve, VE.tensor_tensor(tern[pbuf][:], qc1[:], qc2[:], Alu.subtract), "p3q")
        return mk_tern[j]

    def transpose_16(src_bf16, dst_view_fn, evsem, b0=6, nb=2):
        for k4 in range(0, KT, 4):
            bidx = b0 + (k4 // 4) % nb
            bank = pb[bidx]
            B.guard_bank(te, bidx)
            for kk in range(4):
                k = k4 + kk
                B.i(te, TE.matmul(bank[:, 128 * kk:128 * (kk + 1)], src_bf16[:, 128 * k:128 * (k + 1)], ident[:], start=True, stop=True), "p3tp")
            mk_tp = B.sems["p3tp"].mark()
            B.w(sc, mk_tp)
            mk_ev = B.i(sc, SC.copy(dst_view_fn(k4), bank[:].rearrange("p (k t) -> p k t", t=128)), evsem)
            B.bank_guard[bidx] = mk_ev
        return B.sems["p3tp"].mark(), B.sems[evsem].mark()

    fw_next = [0]
    mk_fwabs = [None] * KT

    def pump_fw(kmax):
        while fw_next[0] < min(kmax, KT):
            j = fw_next[0]
            if j > 0:
                B.w(gp, mk_fwabs[j - 1])
            mk_ld = B.i(gp, GP.dma_start(out=fwb[:], in_=ffnw_d[128 * j:128 * (j + 1), :]), "fwld", 16)
            B.w(ve, mk_ld)
            B.f(ve, VE.reduce_sum(TMP0, fwb[:], axis=AX, apply_absolute_value=True), "fwabs")
            if j == 0:
                mk_fwabs[j] = B.f(ve, VE.tensor_copy(ACC_F, TMP0), "fwabs")
            else:
                mk_fwabs[j] = B.f(ve, VE.tensor_tensor(ACC_F, ACC_F, TMP0, Alu.add), "fwabs")
            fw_next[0] += 1

    mk_qk_used = {}
    mk_qev_free = {0: None, 1: None}
    mk_wqt = {}

    def prep_qk(f):
        quant_stream(f, qkvw_d[128 * f:128 * (f + 1), :], THRQ, NTHRQ,
                     extra_v_wait=(mk_qthr if f == 0 else None))
        B.w(te, mk_tern[f])
        wq = wqt[f % 2]
        if f >= 2:
            B.w(sc, mk_qk_used[f - 2])
        _, mk_wqt[f] = transpose_16(tern[f % 2][:], lambda k4: wq[:].rearrange("p (k t) -> p k t", t=128)[:, k4:k4 + 4, :], "p3wev", b0=0)
        tern_free[f % 2] = B.sems["p3tp"].mark()

    def mm_qk(f):
        wq = wqt[f % 2]
        B.w(te, mk_wqt[f])
        if f == 0:
            B.w(te, mk_hfm[nch - 1])
        for g in range(NG):
            bidx = 4 + g % 2
            bank = pb[bidx]
            B.guard_bank(te, bidx)
            for k in range(KT):
                B.i(te, TE.matmul(bank[:, 0:GSZ], wq[:, 128 * k:128 * (k + 1)], hfm[:, NT * k + GSZ * g: NT * k + GSZ * (g + 1)], start=(k == 0), stop=(k == KT - 1)), "p3mm")
            mk_mm = B.sems["p3mm"].mark()
            if g == NG - 1:
                mk_qk_used[f] = mk_mm
            B.w(sc, mk_mm)
            B.w(sc, mk_qev_free[g % 2])
            if f == 0 and g == 0:
                B.w(sc, mk_qthr)
            mk_e = B.i(sc, SC.mul(qev[g % 2][:, 0:GSZ], bank[:, 0:GSZ], MAGQ), "p3qev")
            B.bank_guard[bidx] = mk_e
            B.w(gp, mk_e)
            mk_qev_free[g % 2] = B.i(gp, GP.dma_start(out=qk_s[f, :, GSZ * g:GSZ * (g + 1)], in_=qev[g % 2][:, 0:GSZ]), f"p3qwr{g % 2}", 16)

    prep_qk(0)
    for f in range(2 * NH):
        if f + 1 < 2 * NH:
            prep_qk(f + 1)
        mm_qk(f)
        pump_fw((f + 1) // 2)

    mk_vev_free = dict(mk_qev_free)
    for vg in range(4):
        for fb in range(4):
            f = 2 * NH + 4 * vg + fb
            quant_stream(f, qkvw_d[128 * f:128 * (f + 1), :], THRQ, NTHRQ)
            B.w(te, mk_tern[f])
            if fb == 0 and vg > 0:
                B.w(sc, mk_v_used)
            _, mk_wvte = transpose_16(
                tern[f % 2][:],
                lambda k4: wvt[:].rearrange("p (k t) -> p k t", t=512)[:, k4:k4 + 4, 128 * fb:128 * (fb + 1)],
                "p3wev", b0=0)
            tern_free[f % 2] = B.sems["p3tp"].mark()
        B.w(te, mk_wvte)
        for c in range(nch):
            bidx = 4 + c % 2
            bank = pb[bidx]
            B.guard_bank(te, bidx)
            for k in range(KT):
                B.i(te, TE.matmul(bank[:, 0:512], hfm[:, NT * k + CH * c: NT * k + CH * (c + 1)], wvt[:, 512 * k:512 * (k + 1)], start=(k == 0), stop=(k == KT - 1)), "p3mm")
            mk_mm = B.sems["p3mm"].mark()
            if c == nch - 1:
                mk_v_used = mk_mm
            B.w(sc, mk_mm)
            B.w(sc, mk_vev_free[c % 2])
            mk_e = B.i(sc, SC.mul(qev[c % 2][:, 0:512], bank[:, 0:512], MAGQ), "p3qev")
            B.bank_guard[bidx] = mk_e
            B.w(gp, mk_e)
            mk_vev_free[c % 2] = B.i(gp, GP.dma_start(out=v_s[vg, c, :, :], in_=qev[c % 2][:, 0:512]), f"p3qwr{c % 2}", 16)
        pump_fw(8 + 2 * (vg + 1))

    mk_pwabs = [None] * KT
    for j in range(KT):
        B.w(sy, mk_pwabs[j - 2] if j >= 2 else wbuf_free[j % 2])
        mk_ld = B.i(sy, SY.dma_start(out=wbuf[j % 2][:], in_=projw_d[128 * j:128 * (j + 1), :]), f"p3pld{j % 2}", 16)
        B.w(ve, mk_ld)
        B.f(ve, VE.reduce_sum(TMP0, wbuf[j % 2][:], axis=AX, apply_absolute_value=True), "ppabs")
        if j == 0:
            mk_pwabs[j] = B.f(ve, VE.tensor_copy(ACC_P, TMP0), "ppabs")
        else:
            mk_pwabs[j] = B.f(ve, VE.tensor_tensor(ACC_P, ACC_P, TMP0, Alu.add), "ppabs")
        wbuf_free[j % 2] = mk_pwabs[j]
    mk_par = part_sum(ACC_P, TMP1, mk_pwabs[KT - 1])
    B.w(ve, mk_par)
    B.f(ve, VE.tensor_scalar(TMP1, TMP1, 1.0 / (DM * DM), None, Alu.mult), "ppq")
    B.f(ve, VE.tensor_scalar(TMP1, TMP1, 1e-5, None, Alu.max), "ppq")
    B.f(ve, VE.tensor_scalar(THRP, TMP1, 0.5, None, Alu.mult), "ppq")
    B.f(ve, VE.tensor_scalar(NTHRP, TMP1, -0.5, None, Alu.mult), "ppq")
    B.f(ve, VE.reciprocal(TMP0, TMP1), "ppq")
    B.f(ve, VE.reciprocal(MAGP, TMP0), "ppq")
    mk_pthr = vpub()
    if stop_after == "P3":
        B.barrier([gp])
        B.audit()
        return nc, ctx, None
    B.barrier([sy, sc, ve, te, gp])
    st3.close()
    hfmst.close()

    wptst = contextlib.ExitStack()
    ctx.enter_context(wptst)
    wpt = sb(wptst, "wpt", [128, KT * DM], BF16)
    st3b = contextlib.ExitStack()
    ctx.enter_context(st3b)
    wbuf = [sb(st3b, f"wbufP{i}", [128, DM], F32) for i in range(2)]
    qc1 = sb(st3b, "qc1P", [128, DM], BF16)
    qc2 = sb(st3b, "qc2P", [128, DM], BF16)
    tern = [sb(st3b, f"ternP{i}", [128, DM], BF16) for i in range(2)]
    mk_tern = {}
    tern_free = {0: None, 1: None}
    wbuf_free = {0: None, 1: None}
    for j in range(KT):
        quant_stream(100 + j, projw_d[128 * j:128 * (j + 1), :], THRP, NTHRP,
                     extra_v_wait=(mk_pthr if j == 0 else None))
        B.w(te, mk_tern[100 + j])
        _, mk_wptev = transpose_16(
            tern[j % 2][:],
            lambda k4: wpt[:].rearrange("p (k t) -> p k t", t=DM)[:, k4:k4 + 4, 128 * j:128 * (j + 1)],
            "p3wev", b0=0)
        tern_free[j % 2] = B.sems["p3tp"].mark()
    mk_wpt = B.sems["p3wev"].mark()
    pump_fw(KT)
    B.barrier([sy, sc, ve, te, gp])
    st3b.close()
    mk_qkv_written = B.sem("p3qwr0").mark()
    mk_qkv_written1 = B.sem("p3qwr1").mark()

    mk_far = part_sum(ACC_F, TMP1, mk_fwabs[KT - 1])
    B.w(ve, mk_far)
    B.f(ve, VE.tensor_scalar(TMP1, TMP1, 1.0 / (DM * DM), None, Alu.mult), "fq")
    B.f(ve, VE.tensor_tensor(WSF, TMP1, EPS_6, Alu.max), "fq")
    B.f(ve, VE.tensor_scalar(THRF, WSF, 0.5, None, Alu.mult), "fq")
    B.f(ve, VE.tensor_scalar(NTHRF, WSF, -0.5, None, Alu.mult), "fq")
    mk_fthr = vpub()


    st2b = contextlib.ExitStack()
    ctx.enter_context(st2b)
    wft = sb(st2b, "wft", [128, KT * DM], BF16)
    wbufB = [sb(st2b, f"wbufB{i}", [128, DM], F32) for i in range(2)]
    qc1B = sb(st2b, "qc1B", [128, DM], BF16)
    qc2B = sb(st2b, "qc2B", [128, DM], BF16)
    ternB = [sb(st2b, f"ternB{i}", [128, DM], BF16) for i in range(2)]
    xqb = [sb(st2b, f"xqb{i}", [128, DM], BF16) for i in range(2)]
    gsb = sb(st2b, "gsb", [128, 1024], F32)

    mk_tB = {}
    mk_tB_used = {}
    mk_c2B = [None, None]
    for j in range(KT):
        pbuf = j % 2
        if mk_c2B[pbuf] is not None:
            B.w(sy, mk_c2B[pbuf])
        mk_ld = B.i(sy, SY.dma_start(out=wbufB[pbuf][:], in_=ffnw_d[128 * j:128 * (j + 1), :]), f"fqld{pbuf}", 16)
        B.w(ve, mk_ld)
        if j == 0:
            B.w(ve, mk_fthr)
        if j - 2 in mk_tB_used:
            B.w(ve, mk_tB_used[j - 2])
        B.f(ve, VE.tensor_scalar(qc1B[:], wbufB[pbuf][:], THRF, None, Alu.is_gt), "fqq")
        mk_c2B[pbuf] = B.f(ve, VE.tensor_scalar(qc2B[:], wbufB[pbuf][:], NTHRF, None, Alu.is_lt), "fqq")
        mk_tB[j] = B.f(ve, VE.tensor_tensor(ternB[pbuf][:], qc1B[:], qc2B[:], Alu.subtract), "fqq")
        B.w(te, mk_tB[j])
        for k4 in range(0, KT, 4):
            bidx = 6 + (k4 // 4) % 2
            bank = pb[bidx]
            B.guard_bank(te, bidx)
            for kk in range(4):
                k = k4 + kk
                B.i(te, TE.matmul(bank[:, 128 * kk:128 * (kk + 1)], ternB[pbuf][:, 128 * k:128 * (k + 1)], ident[:], start=True, stop=True), "fqtp")
            mk_tp = B.sems["fqtp"].mark()
            B.w(sc, mk_tp)
            dst = wft[:].rearrange("p (k t) -> p k t", t=DM)[:, k4:k4 + 4, 128 * j:128 * (j + 1)]
            mk_ev = B.i(sc, SC.copy(dst, bank[:].rearrange("p (k t) -> p k t", t=128)), "fwev")
            B.bank_guard[bidx] = mk_ev
        mk_tB_used[j] = B.sems["fqtp"].mark()
    mk_wft = B.sems["fwev"].mark()

    mk_gw_free = {0: None, 1: None}
    mk_xqld = [None] * nch
    for c in range(nch):
        pbuf = c % 2
        if c >= 2:
            B.w(sy, mk_xq_used[c - 2])
        mk_xqld[c] = B.i(sy, SY.dma_start(out=xqb[pbuf][:], in_=xq_s[c, :, :]), f"xqld{pbuf}", 16)
        B.w(ve, mk_fthr)
        if c > 0:
            B.w(ve, mk_gev)
        B.f(ve, VE.scalar_tensor_tensor(st_f, colw[:, 32 + c:33 + c], R127, WSF, Alu.mult, Alu.mult), "p2bv")
        mk_f = vpub()
        B.w(te, mk_xqld[c])
        if c == 0:
            B.w(te, mk_wft)
        for o in range(4):
            bidx = 4 + o % 2
            bank = pb[bidx]
            B.guard_bank(te, bidx)
            for k in range(KT):
                B.i(te, TE.matmul(bank[:, 0:512], xqb[pbuf][:, 128 * k:128 * (k + 1)], wft[:, DM * k + 512 * o: DM * k + 512 * (o + 1)], start=(k == 0), stop=(k == KT - 1)), "fmm")
            mk_mm = B.sems["fmm"].mark()
            if o == 3:
                if c == 0:
                    mk_xq_used = {}
                mk_xq_used[c] = mk_mm
            B.w(sc, mk_mm)
            B.w(sc, mk_f)
            B.w(sc, mk_gw_free[o % 2])
            mk_gev = B.i(sc, SC.activation(gsb[:, 512 * (o % 2):512 * (o % 2 + 1)], bank[:, 0:512], Act.Silu, scale=st_f), "fgev")
            B.bank_guard[bidx] = mk_gev
            B.w(gp, mk_gev)
            mk_gw_free[o % 2] = B.i(gp, GP.dma_start(out=g_s[c, :, 512 * o:512 * (o + 1)], in_=gsb[:, 512 * (o % 2):512 * (o % 2 + 1)]), f"gwr{o % 2}", 16)
    mk_g_written = B.sem("gwr0").mark()
    mk_g_written1 = B.sem("gwr1").mark()

    if stop_after == "P2b":
        B.barrier([gp])
        B.audit()
        return nc, ctx, None
    B.barrier([sy, sc, ve, te, gp])
    st2b.close()

    st4 = contextlib.ExitStack()
    ctx.enter_context(st4)
    HGS = min(256, NT)
    qkb = sb(st4, "qkb", [128, 2 * NH * HGS], BF16)
    vb = sb(st4, "vb", [128, DM], BF16)
    gb = sb(st4, "gb", [128, DM], F32)
    xb4 = sb(st4, "xb4", [128, DM], F32)
    aob = sb(st4, "aob", [128, DM], F32)
    owb = sb(st4, "owb", [128, DM], F32)
    r2tm = sb(st4, "r2tm", [128, DM], BF16)
    r2fm = sb(st4, "r2fm", [128, DM], BF16)
    sgb = sb(st4, "sgb", [128, DM], F32)
    outb = sb(st4, "outb", [128, DM], F32)
    S_f = sb(st4, "S_f", [128, NH * 128], F32)
    S_b = sb(st4, "S_b", [128, NH * 128], BF16)
    smt = sb(st4, "smt", [128, 512], BF16)
    qlm = sb(st4, "qlm", [128, 512], BF16)
    ktm = sb(st4, "ktm", [128, 512], BF16)
    lamt = sb(st4, "lamt", [128, NH * 128], BF16)
    maskt = sb(st4, "maskt", [128, NH * 128], F32)
    lamrevw = sb(st4, "lamrevw", [128, NH * 128], F32)
    lamTw = sb(st4, "lamTw", [128, NH * 128], F32)
    nwb = sb(st4, "nwb", [128, 2 * DM], F32)
    qk3 = qkb[:].rearrange("p (f x) -> p f x", x=HGS)

    mk_n0 = B.i(sy, SY.dma_start(out=nwb[:, 0:DM], in_=nw_d[:]), "p4c", 16)
    mk_nwb = B.i(sy, SY.dma_start(out=nwb[:, DM:2 * DM], in_=pn_d[:]), "p4c", 16)
    for h in range(NH):
        nsh = svec[:, NH + h:NH + h + 1]
        B.i(sc, SC.activation(maskt[:, 128 * h:128 * (h + 1)], cons[:, 128:256], Act.Exp, scale=nsh), "p4dc")
        B.i(sc, SC.activation(lamt[:, 128 * h:128 * (h + 1)], cons[:, 0:128], Act.Exp, scale=nsh), "p4dc")
    mk_dc = spub()
    B.w(ve, mk_dc)
    B.w(ve, mk_ones)
    for h in range(NH):
        B.f(ve, VE.tensor_tensor(maskt[:, 128 * h:128 * (h + 1)], maskt[:, 128 * h:128 * (h + 1)], cons[:, 256:384], Alu.mult), "p4dcv")
        B.f(ve, VE.tensor_scalar(lamrevw[:, 128 * h:128 * (h + 1)], ones[:], LAMREV(h), None, Alu.mult), "p4dcv")
        B.f(ve, VE.tensor_scalar(lamTw[:, 128 * h:128 * (h + 1)], ones[:], LAMT(h), None, Alu.mult), "p4dcv")
    mk_masks = vpub()

    B.w(sy, mk_qkv_written)
    B.w(sy, mk_qkv_written1)
    B.w(sy, mk_g_written)
    B.w(sy, mk_g_written1)
    mk_out = None
    mk_smt_ev = None
    mk_qlm_ev = None
    mk_ktm_ev = None
    mk_o_ev = None
    mk_Sb = {}
    mk_Su = None
    for c in range(nch):
        hg, hh = (c * CH) // HGS, ((c * CH) % HGS) // CH
        if hh == 0:
            if hg > 0:
                B.w(sy, mk_hg_used)
            for f in range(2 * NH):
                B.i(sy, SY.dma_start(out=qkb[:, HGS * f:HGS * (f + 1)], in_=qk_s[f, :, HGS * hg:HGS * (hg + 1)]), "p4qk", 16)
            mk_qk_all = B.sems["p4qk"].mark()
        if c > 0:
            B.w(sy, mk_vb_used)
        for vg in range(4):
            mk_vld = B.i(sy, SY.dma_start(out=vb[:, 512 * vg:512 * (vg + 1)], in_=v_s[vg, c, :, :]), "p4v", 16)
        if c > 0:
            B.w(sy, mk_gb_used)
        mk_gld = B.i(sy, SY.dma_start(out=gb[:], in_=g_s[c, :, :]), "p4g", 16)
        if c > 0:
            B.w(sy, mk_xb4_used)
        mk_x4 = B.i(sy, SY.dma_start(out=xb4[:], in_=x_d[CH * c:CH * (c + 1), :]), "p4x", 16)

        for g in range(4):
            heads = list(range(4 * g, 4 * g + 4))
            wide = slice(512 * g, 512 * (g + 1))
            B.w(te, mk_qk_all)
            B.guard_bank(te, 3)
            for i, h in enumerate(heads):
                k_t = qk3[:, NH + h, 128 * hh:128 * hh + 128]
                B.i(te, TE.matmul(pb[3][:, 128 * i:128 * (i + 1)], k_t, ident[:], start=(i == 0), stop=(i == 3)), "p4tp")
            mk_tpk = B.sems["p4tp"].mark()
            B.w(ve, mk_tpk)
            if c == 0 and g == 0:
                B.w(ve, mk_lamT)
                B.w(ve, mk_masks)
            mk_ktm_ev = B.f(ve, VE.tensor_tensor(ktm[:], pb[3][:, 0:512], lamrevw[:, wide], Alu.mult), "p4ktm")
            B.bank_guard[3] = mk_ktm_ev
            B.guard_bank(te, 0)
            for i, h in enumerate(heads):
                k_t = qk3[:, NH + h, 128 * hh:128 * hh + 128]
                q_t = qk3[:, h, 128 * hh:128 * hh + 128]
                B.i(te, TE.matmul(pb[0][:, 128 * i:128 * (i + 1)], k_t, q_t, start=(i == 0), stop=(i == 3)), "p4mm")
            mk_sct = B.sems["p4mm"].mark()
            B.w(ve, mk_sct)
            mk_smt_ev = B.f(ve, VE.tensor_tensor(smt[:], pb[0][:, 0:512], maskt[:, wide], Alu.mult), "p4smt")
            B.bank_guard[0] = mk_smt_ev
            mk_qlm_ev = B.f(ve, VE.tensor_tensor(qlm[:], qk3[:, 4 * g:4 * g + 4, 128 * hh:128 * hh + 128], lamt[:, wide], Alu.mult), "p4qlm")
            B.w(te, mk_smt_ev)
            B.w(te, mk_vld)
            B.guard_bank(te, 1)
            if c > 0:
                B.w(te, mk_qlm_ev)
                B.w(te, mk_Sb[g])
            for i, h in enumerate(heads):
                v_t = vb[:, 128 * h:128 * (h + 1)]
                B.i(te, TE.matmul(pb[1][:, 128 * i:128 * (i + 1)], smt[:, 128 * i:128 * (i + 1)], v_t, start=(i == 0), stop=(c == 0 and i == 3)), "p4mm")
                if c > 0:
                    B.i(te, TE.matmul(pb[1][:, 128 * i:128 * (i + 1)], qlm[:, 128 * i:128 * (i + 1)], S_b[:, 128 * h:128 * (h + 1)], start=False, stop=(i == 3)), "p4mm")
            mk_o = B.sems["p4mm"].mark()
            B.w(sc, mk_o)
            if c > 0 and g == 0:
                B.w(sc, mk_ow_ev)
            mk_o_ev = B.i(sc, SC.mul(aob[:, wide], pb[1][:, 0:512], OSC), "p4aev")
            B.bank_guard[1] = mk_o_ev
            B.w(te, mk_ktm_ev)
            B.guard_bank(te, 2)
            for i, h in enumerate(heads):
                v_t = vb[:, 128 * h:128 * (h + 1)]
                B.i(te, TE.matmul(pb[2][:, 128 * i:128 * (i + 1)], ktm[:, 128 * i:128 * (i + 1)], v_t, start=(i == 0), stop=(i == 3)), "p4mm")
            mk_kv = B.sems["p4mm"].mark()
            B.w(ve, mk_kv)
            if c == 0:
                mk_Su = B.f(ve, VE.tensor_copy(S_f[:, wide], pb[2][:, 0:512]), "p4su")
            else:
                B.f(ve, VE.tensor_tensor(S_f[:, wide], S_f[:, wide], lamTw[:, wide], Alu.mult), "p4su")
                mk_Su = B.f(ve, VE.tensor_tensor(S_f[:, wide], S_f[:, wide], pb[2][:, 0:512], Alu.add), "p4su")
            B.bank_guard[2] = mk_Su
            mk_Sb[g] = B.f(ve, VE.tensor_copy(S_b[:, wide], S_f[:, wide]), "p4sb")
        if hh == (HGS // CH) - 1:
            mk_hg_used = B.sems["p4mm"].mark()
        mk_vb_used = B.sems["p4mm"].mark()

        mk_ao = B.sems["p4aev"].mark()
        B.w(sc, mk_ao)
        if c > 0:
            B.w(sc, mk_xb4_used)
        B.f(sc, SC.activation(sgb[:], aob[:], Act.Square, accum_out=st_sum), "p4s")
        B.f(sc, SC.activation(st_t0, st_sum, Act.Ln, scale=1.0 / DM, bias=EPS_6), "p4s")
        B.f(sc, SC.activation(st_r, st_t0, Act.Exp, scale=-0.5), "p4s")
        mk_r1 = spub()
        B.w(gp, mk_r1)
        B.w(gp, mk_nwb)
        if c > 0:
            B.w(gp, mk_ow_used)
        mk_ow_ev = B.f(gp, GP.tensor_tensor(owb[:], aob[:], nwb[:, 0:DM], Alu.mult), "p4ow")
        B.w(sc, mk_ow_ev)
        B.f(sc, SC.activation(sgb[:], owb[:], Act.Square, accum_out=st_ssq), "p4s")
        mk_s2 = spub()
        B.w(ve, mk_s2)
        B.f(ve, VE.tensor_scalar(st_t0, st_ssq, 1.0 / DM, None, Alu.mult), "p4ve")
        B.f(ve, VE.tensor_tensor(st_t1, st_t0, st_r, Alu.mult), "p4ve")
        B.f(ve, VE.tensor_tensor(st_t1, st_t1, st_r, Alu.mult), "p4ve")
        mk_mm2b = vpub()
        B.w(sc, mk_mm2b)
        B.f(sc, SC.activation(st_t0, st_t1, Act.Ln, bias=EPSM_AD), "p4s")
        B.f(sc, SC.activation(st_r2, st_t0, Act.Exp, scale=-0.5), "p4s")
        mk_r2b = spub()
        B.w(ve, mk_r2b)
        B.f(ve, VE.tensor_tensor(st_rr, st_r, st_r2, Alu.mult), "p4ve")
        mk_r12 = vpub()
        B.w(ve, mk_r12)
        if c > 0:
            B.w(ve, mk_r2tm_used)
        B.f(ve, VE.scalar_tensor_tensor(r2tm[:], owb[:], st_rr, nwb[:, DM:2 * DM], Alu.mult, Alu.mult), "p4r2")
        mk_r2tm = vpub()
        mk_ow_used = mk_r2tm

        B.w(te, mk_r2tm)
        if c > 0:
            B.w(sc, mk_r2fm_used)
        for k4 in range(0, KT, 4):
            bidx = 6 + (k4 // 4) % 2
            bank = pb[bidx]
            B.guard_bank(te, bidx)
            for kk in range(4):
                k = k4 + kk
                B.i(te, TE.matmul(bank[:, 128 * kk:128 * (kk + 1)], r2tm[:, 128 * k:128 * (k + 1)], ident[:], start=True, stop=True), "p4tp")
            mk_tp = B.sems["p4tp"].mark()
            B.w(sc, mk_tp)
            mk_ev = B.i(sc, SC.copy(r2fm[:].rearrange("p (k t) -> p k t", t=128)[:, k4:k4 + 4, :], bank[:].rearrange("p (k t) -> p k t", t=128)), "p4rev")
            B.bank_guard[bidx] = mk_ev
        mk_r2tm_used = B.sems["p4tp"].mark()
        mk_r2fm = B.sems["p4rev"].mark()

        B.w(te, mk_r2fm)
        if c == 0:
            B.w(te, mk_wpt)
        for o in range(4):
            bidx = 4 + o % 2
            bank = pb[bidx]
            B.guard_bank(te, bidx)
            for k in range(KT):
                B.i(te, TE.matmul(bank[:, 0:512], r2fm[:, 128 * k:128 * (k + 1)], wpt[:, DM * k + 512 * o: DM * k + 512 * (o + 1)], start=(k == 0), stop=(k == KT - 1)), "p4pm")
            mk_pm = B.sems["p4pm"].mark()
            if o == 3:
                mk_r2fm_used = mk_pm
            B.w(ve, mk_pm)
            B.w(ve, mk_gld)
            mk_sg = B.f(ve, VE.scalar_tensor_tensor(sgb[:, 512 * o:512 * (o + 1)], bank[:, 0:512], MAGP, gb[:, 512 * o:512 * (o + 1)], Alu.mult, Alu.mult), "p4sg")
            B.bank_guard[bidx] = mk_sg
            B.w(gp, mk_sg)
            B.w(gp, mk_x4)
            if c > 0:
                B.w(gp, mk_ow_free[o])
            mk_oc = B.f(gp, GP.tensor_tensor(outb[:, 512 * o:512 * (o + 1)], sgb[:, 512 * o:512 * (o + 1)], xb4[:, 512 * o:512 * (o + 1)], Alu.add), "p4oc")
            mk_out = B.i(gp, GP.dma_start(out=y_d[CH * c:CH * (c + 1), 512 * o:512 * (o + 1)], in_=outb[:, 512 * o:512 * (o + 1)]), f"yout{o}", 16)
            if c == 0 and o == 0:
                mk_ow_free = {}
            mk_ow_free[o] = mk_out
        mk_gb_used = mk_sg
        mk_xb4_used = mk_oc

    B.barrier([gp])
    B.audit()
    return nc, ctx, None


def _consts():
    t = np.arange(128, dtype=np.float32)
    tt = np.tile(t[None, :], (128, 1))
    diff = t[None, :] - t[:, None]
    diffc = np.maximum(diff, 0.0).astype(np.float32)
    lmask = (diff >= 0).astype(np.float32)
    tmin = (128.0 - t)[:, None].astype(np.float32)
    eps = np.zeros((128, 8), np.float32)
    eps[:, 0] = 1e-5
    eps[:, 1] = np.finfo(np.float32).eps
    eps[:, 2] = 1e-8
    eps[:, 3] = 1e-6
    ident = np.eye(128, dtype=ml_dtypes.bfloat16)
    return {"c_ident": ident, "c_tt": tt.astype(np.float32), "c_diffc": diffc,
            "c_lmask": lmask, "c_tmin": tmin, "c_eps": eps}


def kernel(L0_attn_x, L0_attn_qkv_w, L0_attn_qkv_norm_w, L0_attn_proj_w,
           L0_attn_proj_norm_w, L0_attn_norm_w, L0_attn_s, L0_ffn_weight):
    x = np.ascontiguousarray(np.asarray(L0_attn_x, dtype=np.float32))
    nb = x.shape[0]
    shared = {
        "qkv_w": np.ascontiguousarray(np.asarray(L0_attn_qkv_w, np.float32)),
        "proj_w": np.ascontiguousarray(np.asarray(L0_attn_proj_w, np.float32)),
        "ffn_w": np.ascontiguousarray(np.asarray(L0_ffn_weight, np.float32)),
        "qkv_norm_w": np.tile(np.asarray(L0_attn_qkv_norm_w, np.float32).reshape(1, DM), (128, 1)),
        "proj_norm_w": np.tile(np.asarray(L0_attn_proj_norm_w, np.float32).reshape(1, DM), (128, 1)),
        "norm_w": np.tile(np.asarray(L0_attn_norm_w, np.float32).reshape(1, DM), (128, 1)),
        "s_dec": np.tile(np.asarray(L0_attn_s, np.float32).reshape(1, NH), (128, 1)),
        **_consts(),
    }
    nc, ctx, _ = build_program(nch=x.shape[1] // CH)
    with ctx:
        in_maps = [dict(shared, x=np.ascontiguousarray(x[b])) for b in range(nb)]
        res = run_bass_kernel_spmd(nc, in_maps, core_ids=list(range(nb)))
    return np.stack([np.asarray(r["y"], np.float32) for r in res.results], axis=0)


# version 11
# speedup vs baseline: 1.0519x; 1.0519x over previous
import contextlib
import numpy as np
import ml_dtypes
import concourse.bass as bass
import concourse.mybir as mybir
from concourse.bass_utils import run_bass_kernel_spmd
from concourse.bass_isa import ReduceOp

F32 = mybir.dt.float32
BF16 = mybir.dt.bfloat16
I32 = mybir.dt.int32
Alu = mybir.AluOpType
Act = mybir.ActivationFunctionType
AX = mybir.AxisListType.X

DM = 2048
NH = 16
CH = 128
KT = DM // 128
SCALE_O = float(128) ** -0.5


class SemT:
    def __init__(self, bld, name):
        self.name = name
        self.h = bld.ctx.enter_context(bld.nc.semaphore(name))
        self.v = 0

    def mark(self):
        return (self, self.v)


class Eng:
    def __init__(self, name, h):
        self.name = name
        self.h = h


class Bld:
    def __init__(self, nc, ctx):
        self.nc = nc
        self.ctx = ctx
        self.sems = {}
        self.log = {e: [] for e in ("sync", "scalar", "vector", "tensor", "gpsimd")}
        self.bank_guard = {}

    def sem(self, name):
        if name not in self.sems:
            self.sems[name] = SemT(self, name)
        return self.sems[name]

    def w(self, eng, mark):
        if mark is None:
            return
        if isinstance(mark, tuple) and len(mark) == 2 and isinstance(mark[0], tuple):
            self.w(eng, mark[0]); self.w(eng, mark[1]); return
        sem, val = mark
        if val <= 0:
            return
        eng.h.wait_ge(sem.h, val)
        self.log[eng.name].append(("wait", sem.name, val))

    def i(self, eng, instr, semname, n=1):
        sem = self.sem(semname)
        instr.then_inc(sem.h, n)
        sem.v += n
        self.log[eng.name].append(("inc", sem.name, n))
        return sem.mark()

    def f(self, eng, instr, semname):
        mk = self.i(eng, instr, semname)
        self.w(eng, mk)
        return mk

    def guard_bank(self, te, idx):
        self.w(te, self.bank_guard.get(idx))

    def barrier(self, engs):
        marks = [(s, s.v) for s in list(self.sems.values())]
        for e in engs:
            for mk in marks:
                self.w(e, mk)

    def audit(self):
        ptr = {e: 0 for e in self.log}
        vals = {}
        progress = True
        while progress:
            progress = False
            for e, evs in self.log.items():
                while ptr[e] < len(evs):
                    kind, s, n = evs[ptr[e]]
                    if kind == "wait" and vals.get(s, 0) < n:
                        break
                    if kind == "inc":
                        vals[s] = vals.get(s, 0) + n
                    ptr[e] += 1
                    progress = True
        stuck = {e: (ptr[e], len(evs), evs[ptr[e]]) for e, evs in self.log.items()
                 if ptr[e] < len(evs)}
        if stuck:
            raise RuntimeError(f"deadlock audit failed: {stuck}")


def build_program(nch=16, stop_after="P4", dbg_scratch=False):
    NT = nch * CH
    GSZ = 512 if NT >= 512 else NT
    NG = NT // GSZ
    nc = bass.Bass("TRN2", target_bir_lowering=False)
    ctx = contextlib.ExitStack()
    B = Bld(nc, ctx)
    sy = Eng("sync", nc.sync)
    sc = Eng("scalar", nc.scalar)
    ve = Eng("vector", nc.vector)
    te = Eng("tensor", nc.tensor)
    gp = Eng("gpsimd", nc.gpsimd)
    SY, SC, VE, TE, GP = nc.sync, nc.scalar, nc.vector, nc.tensor, nc.gpsimd

    x_d = nc.dram_tensor("x", [NT, DM], F32, kind="ExternalInput").ap()
    qkvw_d = nc.dram_tensor("qkv_w", [3 * DM, DM], F32, kind="ExternalInput").ap()
    projw_d = nc.dram_tensor("proj_w", [DM, DM], F32, kind="ExternalInput").ap()
    ffnw_d = nc.dram_tensor("ffn_w", [DM, DM], F32, kind="ExternalInput").ap()
    qn_d = nc.dram_tensor("qkv_norm_w", [128, DM], F32, kind="ExternalInput").ap()
    pn_d = nc.dram_tensor("proj_norm_w", [128, DM], F32, kind="ExternalInput").ap()
    nw_d = nc.dram_tensor("norm_w", [128, DM], F32, kind="ExternalInput").ap()
    s_d = nc.dram_tensor("s_dec", [128, NH], F32, kind="ExternalInput").ap()
    ident_d = nc.dram_tensor("c_ident", [128, 128], BF16, kind="ExternalInput").ap()
    tt_d = nc.dram_tensor("c_tt", [128, 128], F32, kind="ExternalInput").ap()
    diffc_d = nc.dram_tensor("c_diffc", [128, 128], F32, kind="ExternalInput").ap()
    lmask_d = nc.dram_tensor("c_lmask", [128, 128], F32, kind="ExternalInput").ap()
    tmin_d = nc.dram_tensor("c_tmin", [128, 1], F32, kind="ExternalInput").ap()
    eps_d = nc.dram_tensor("c_eps", [128, 8], F32, kind="ExternalInput").ap()
    y_d = nc.dram_tensor("y", [NT, DM], F32, kind="ExternalOutput").ap()
    skind = dict(kind="ExternalOutput") if dbg_scratch else {}
    g_s = nc.dram_tensor("g_s", [nch, 128, DM], F32, **skind).ap()
    qk_s = nc.dram_tensor("qk_s", [2 * NH, 128, NT], BF16, **skind).ap()
    v_s = nc.dram_tensor("v_s", [4, nch, 128, 512], BF16, **skind).ap()
    xq_s = nc.dram_tensor("xq_s", [nch, 128, DM], BF16, **skind).ap()
    h_sdbg = nc.dram_tensor("h_sdbg", [nch, 128, DM], BF16, kind="ExternalOutput").ap() if dbg_scratch else None
    ao_dbg = nc.dram_tensor("ao_dbg", [128, DM], F32, kind="ExternalOutput").ap() if dbg_scratch else None
    cw_dbg = nc.dram_tensor("cw_dbg", [128, 64], F32, kind="ExternalOutput").ap() if dbg_scratch else None
    r2_dbg = nc.dram_tensor("r2_dbg", [128, DM], BF16, kind="ExternalOutput").ap() if dbg_scratch else None
    sg_dbg = nc.dram_tensor("sg_dbg", [128, DM], F32, kind="ExternalOutput").ap() if dbg_scratch else None
    xn_dbg = nc.dram_tensor("xn_dbg", [128, DM], F32, kind="ExternalOutput").ap() if dbg_scratch else None
    z_dbg = nc.dram_tensor("z_dbg", [128, DM], F32, kind="ExternalOutput").ap() if dbg_scratch else None
    st_dbg = nc.dram_tensor("st_dbg", [128, 16], F32, kind="ExternalOutput").ap() if dbg_scratch else None

    sb = lambda st, name, shp, dt: st.enter_context(nc.sbuf_tensor(name, shp, dt))

    ident = sb(ctx, "ident", [128, 128], BF16)
    cons = sb(ctx, "cons", [128, 384], F32)
    tmin = sb(ctx, "tmin", [128, 1], F32)
    eps = sb(ctx, "eps", [128, 8], F32)
    svec = sb(ctx, "svec", [128, NH * 4], F32)
    colw = sb(ctx, "colw", [128, 64], F32)
    stats = sb(ctx, "stats", [128, 16], F32)
    pb = [ctx.enter_context(nc.psum_tensor(f"pb{i}", [128, 512], F32)) for i in range(8)]
    ones = sb(ctx, "ones", [128, 128], F32)
    hfmst = contextlib.ExitStack()
    ctx.enter_context(hfmst)
    hfm = sb(hfmst, "hfm", [128, KT * NT], BF16)

    EPS_LN = eps[:, 0:1]
    EPS_M = eps[:, 1:2]
    EPS_8 = eps[:, 2:3]
    EPS_6 = eps[:, 3:4]
    THRQ, MAGQ = colw[:, 16:17], colw[:, 17:18]
    NTHRQ, NTHRP, NTHRF = colw[:, 27:28], colw[:, 28:29], colw[:, 29:30]
    THRP, MAGP = colw[:, 18:19], colw[:, 19:20]
    THRF, WSF = colw[:, 20:21], colw[:, 21:22]
    ACC_Q = colw[:, 22:23]
    ACC_F = colw[:, 23:24]
    ACC_P = colw[:, 24:25]
    TMP0, TMP1 = colw[:, 25:26], colw[:, 26:27]

    TRASH0, TRASH1 = stats[:, 14:15], stats[:, 15:16]

    def vpub():
        return B.f(ve, VE.tensor_copy(TRASH0, TRASH1), "pubv")

    def spub():
        return B.f(sc, SC.copy(colw[:, 56:57], colw[:, 57:58]), "pubs")

    st_sum, st_m, st_r, st_b, st_v = (stats[:, i:i + 1] for i in range(5))
    st_a, st_q, st_r2, st_rr, st_brr = (stats[:, i:i + 1] for i in range(5, 10))
    st_f, st_t0, st_t1, st_ssq = (stats[:, i:i + 1] for i in range(10, 14))

    B.i(sy, SY.dma_start(out=ident[:], in_=ident_d[:]), "ld0", 16)
    B.i(sy, SY.dma_start(out=cons[:, 0:128], in_=tt_d[:]), "ld0", 16)
    B.i(sy, SY.dma_start(out=cons[:, 128:256], in_=diffc_d[:]), "ld0", 16)
    B.i(sy, SY.dma_start(out=cons[:, 256:384], in_=lmask_d[:]), "ld0", 16)
    B.i(sy, SY.dma_start(out=tmin[:], in_=tmin_d[:]), "ld0", 16)
    B.i(sy, SY.dma_start(out=eps[:], in_=eps_d[:]), "ld0", 16)
    mk_ld0 = B.i(sy, SY.dma_start(out=svec[:, 0:NH], in_=s_d[:]), "ld0", 16)

    B.i(ve, VE.memset(stats[:, 14:16], 0.0), "p0v")
    B.i(ve, VE.memset(colw[:, 56:58], 0.0), "p0v")
    B.f(ve, VE.memset(colw[:, 48:49], float(3 * DM * DM)), "p0v")
    B.f(ve, VE.reciprocal(colw[:, 49:50], colw[:, 48:49]), "p0v")
    B.f(ve, VE.memset(colw[:, 50:51], 127.0), "p0v")
    B.f(ve, VE.reciprocal(colw[:, 51:52], colw[:, 50:51]), "p0v")
    RC3M = colw[:, 49:50]
    R127 = colw[:, 51:52]
    OSC = colw[:, 53:54]
    EPSM_AD = colw[:, 55:56]
    B.f(ve, VE.memset(colw[:, 52:53], 128.0), "p0v")
    B.f(ve, VE.memset(colw[:, 54:55], 3.0), "p0v")
    B.f(ve, VE.reciprocal(EPSM_AD, colw[:, 54:55]), "p0v")
    B.f(ve, VE.tensor_scalar(EPSM_AD, EPSM_AD, 4.0, -1.0, Alu.mult, Alu.add), "p0v")
    B.f(ve, VE.tensor_scalar(EPSM_AD, EPSM_AD, 3.0, -1.0, Alu.mult, Alu.add), "p0v")
    B.f(ve, VE.tensor_scalar(colw[:, 54:55], EPSM_AD, -1.0, None, Alu.mult), "p0v")
    B.f(ve, VE.tensor_tensor(EPSM_AD, EPSM_AD, colw[:, 54:55], Alu.max), "p0v")
    B.w(ve, mk_ld0)
    mk_ones = B.i(ve, VE.memset(ones[:], 1.0), "p0v")
    B.f(ve, VE.tensor_scalar(svec[:, NH:2 * NH], svec[:, 0:NH], -1.0, None, Alu.mult), "p0v")
    B.f(ve, VE.tensor_scalar(svec[:, 2 * NH:3 * NH], svec[:, NH:2 * NH], tmin[:, 0:1], None, Alu.mult), "p0v")
    mk_lr0 = vpub()
    B.w(sc, mk_lr0)
    B.f(sc, SC.activation(svec[:, 2 * NH:3 * NH], svec[:, 2 * NH:3 * NH], Act.Exp), "p0s")
    B.f(sc, SC.activation(svec[:, 3 * NH:4 * NH], svec[:, 0:NH], Act.Exp, scale=-128.0), "p0s")
    B.f(sc, SC.activation(OSC, colw[:, 52:53], Act.Ln), "p0s")
    B.f(sc, SC.activation(OSC, OSC, Act.Exp, scale=-0.5), "p0s")
    mk_lamT = spub()
    LAMREV = lambda h: svec[:, 2 * NH + h:2 * NH + h + 1]
    LAMT = lambda h: svec[:, 3 * NH + h:3 * NH + h + 1]

    if stop_after == "P0":
        B.w(ve, mk_lamT)
        B.barrier([gp])
        B.audit()
        return nc, ctx, None

    st2 = contextlib.ExitStack()
    ctx.enter_context(st2)
    xb = [sb(st2, f"xb{i}", [128, DM], F32) for i in range(2)]
    dbufs = [sb(st2, f"dbuf{i}", [128, DM], F32) for i in range(2)]
    zbufs = [sb(st2, f"zbuf{i}", [128, DM], F32) for i in range(2)]
    uibs = [sb(st2, f"uib{i}", [128, DM], I32) for i in range(2)]
    dmp = sb(st2, "dmp", [128, DM], BF16)
    cc1 = sb(st2, "cc1", [128, DM], BF16)
    cc2 = sb(st2, "cc2", [128, DM], BF16)
    xqtm = [sb(st2, f"xqtm{i}", [128, DM], BF16) for i in range(2)]
    htm = [sb(st2, f"htm{i}", [128, DM], BF16) for i in range(2)]
    xqfm = [sb(st2, f"xqfm{i}", [128, DM], BF16) for i in range(2)]
    qwb = [sb(st2, f"qwb{i}", [128, DM], F32) for i in range(2)]
    qnb = sb(st2, "qnb", [128, DM], F32)
    p2stats = sb(st2, "p2stats", [128, 24], F32)

    mk_qnb = B.i(sy, SY.dma_start(out=qnb[:], in_=qn_d[:]), "qnbld", 16)

    def part_sum(acc_ap, out_ap, wait_mark):
        B.w(te, wait_mark)
        B.w(te, mk_ones)
        B.guard_bank(te, 6)
        mk = B.i(te, TE.matmul(pb[6][:, 0:1], ones[:], acc_ap, start=True, stop=True), "psum_mm")
        B.w(sc, mk)
        mk_ev = B.f(sc, SC.copy(out_ap, pb[6][:, 0:1]), "psum_ev")
        B.bank_guard[6] = mk_ev
        return spub()

    NWQ = 3 * KT
    mk_qwld = [None] * NWQ
    mk_qwabs = [None] * NWQ
    qw_next = [0]

    def pump_qw(kmax):
        while qw_next[0] < min(kmax, NWQ):
            j = qw_next[0]
            if j > 1:
                B.w(gp, mk_qwabs[j - 2])
            mk_qwld[j] = B.i(gp, GP.dma_start(out=qwb[j % 2][:], in_=qkvw_d[128 * j:128 * (j + 1), :]), f"qwld{j % 2}", 16)
            B.w(ve, mk_qwld[j])
            B.f(ve, VE.reduce_sum(TMP0, qwb[j % 2][:], axis=AX, apply_absolute_value=True), "qwabs")
            if j == 0:
                mk_qwabs[j] = B.f(ve, VE.tensor_copy(ACC_Q, TMP0), "qwabs")
            else:
                mk_qwabs[j] = B.f(ve, VE.tensor_tensor(ACC_Q, ACC_Q, TMP0, Alu.add), "qwabs")
            qw_next[0] += 1

    mk_xld = [None] * nch
    mk_xfree = [None] * nch
    mk_hfm = [None] * nch
    mk_xqs = {0: None, 1: None}
    mk_xq_used = {0: None, 1: None}
    mk_htm_used = {0: None, 1: None}
    pend_tp = []
    for c in range(nch):
        p = c % 2
        xbp, dbp, zbp, uibp = xb[p], dbufs[p], zbufs[p], uibs[p]
        SS = lambda i: p2stats[:, 12 * p + i:12 * p + i + 1]
        SA, SB, M, NB, T, V, U, RR, AQ, QP, RL = (SS(i) for i in range(11))
        if c >= 2:
            B.w(sy, mk_xfree[c - 2])
        mk_xld[c] = B.i(sy, SY.dma_start(out=xbp[:], in_=x_d[CH * c:CH * (c + 1), :]), f"xld{p}", 16)
        pump_qw(3 * (c + 1))

        B.w(sc, mk_xld[c])
        B.f(sc, SC.activation(dmp[:], xbp[:], Act.Copy, accum_out=SA), "p2s")
        B.f(sc, SC.activation(dmp[:], xbp[:], Act.Square, accum_out=SB), "p2s")
        mk_acc = spub()
        if pend_tp:
            pend_tp.pop(0)()

        B.w(ve, mk_acc)
        B.f(ve, VE.tensor_scalar(M, SA, 1.0 / DM, None, Alu.mult), "p2v")
        B.f(ve, VE.tensor_scalar(NB, M, -1.0, None, Alu.mult), "p2v")
        B.f(ve, VE.tensor_scalar(T, SB, 1.0 / DM, None, Alu.mult), "p2v")
        B.f(ve, VE.tensor_tensor(U, M, M, Alu.mult), "p2v")
        B.f(ve, VE.tensor_tensor(V, T, U, Alu.subtract), "p2v")
        B.f(ve, VE.tensor_tensor(U, V, EPS_LN, Alu.add), "p2v")
        B.f(ve, VE.tensor_scalar(T, U, EPSM_AD, None, Alu.mult), "p2v")
        B.f(ve, VE.tensor_tensor(T, V, T, Alu.add), "p2v")
        mk_arg = vpub()
        B.w(sc, mk_arg)
        B.f(sc, SC.activation(RL, T, Act.Ln), "p2s")
        B.f(sc, SC.activation(RR, RL, Act.Exp, scale=-0.5), "p2s")
        B.f(sc, SC.activation(RL, U, Act.Ln), "p2s")
        B.f(sc, SC.activation(RL, RL, Act.Exp, scale=-0.5), "p2s")
        mk_rr = spub()

        B.w(ve, mk_rr)
        mk_d = B.f(ve, VE.tensor_scalar(dbp[:], xbp[:], NB, None, Alu.add), "p2v")
        B.f(ve, VE.reduce_max(AQ, dbp[:], axis=AX, apply_absolute_value=True), "p2v")
        B.f(ve, VE.tensor_tensor(AQ, AQ, EPS_8, Alu.max), "p2v")
        B.f(ve, VE.tensor_scalar(colw[:, 32 + c:33 + c], AQ, RL, None, Alu.mult), "p2v")
        B.f(ve, VE.reciprocal(T, AQ), "p2v")
        B.f(ve, VE.tensor_scalar(QP, T, 127.0, None, Alu.mult), "p2v")
        B.w(ve, mk_qnb)
        B.w(ve, mk_htm_used[p])
        B.f(ve, VE.scalar_tensor_tensor(htm[p][:], dbp[:], RR, qnb[:], Alu.mult, Alu.mult), "p2v")
        mk_htm = vpub()
        B.w(sc, mk_htm)
        mk_z = B.f(sc, SC.activation(zbp[:], dbp[:], Act.Copy, scale=QP), "p2s")
        mk_zp = spub()

        B.w(ve, mk_zp)
        B.f(ve, VE.tensor_copy(uibp[:], zbp[:]), "p2v")
        B.f(ve, VE.tensor_copy(xbp[:], uibp[:]), "p2v")
        B.f(ve, VE.scalar_tensor_tensor(cc1[:], zbp[:], 0.5, xbp[:], Alu.subtract, Alu.is_gt), "p2v")
        B.f(ve, VE.scalar_tensor_tensor(cc2[:], zbp[:], 0.5, xbp[:], Alu.add, Alu.is_lt), "p2v")
        B.w(ve, mk_xq_used[p])
        B.f(ve, VE.tensor_tensor(xqtm[p][:], xbp[:], cc1[:], Alu.add), "p2v")
        mk_xq = B.f(ve, VE.tensor_tensor(xqtm[p][:], xqtm[p][:], cc2[:], Alu.subtract), "p2v")
        mk_xfree[c] = mk_xq

        def do_transposes(c=c, p=p, mk_xq=mk_xq, mk_htm=mk_htm):
            B.w(te, mk_xq)
            if c == 0:
                B.w(te, mk_ld0)
            B.w(sc, mk_xqs[p])
            for k4 in range(0, KT, 4):
                bidx = (k4 // 4) % 4
                bank = pb[bidx]
                B.guard_bank(te, bidx)
                for kk in range(4):
                    k = k4 + kk
                    B.i(te, TE.matmul(bank[:, 128 * kk:128 * (kk + 1)], xqtm[p][:, 128 * k:128 * (k + 1)], ident[:], start=True, stop=True), "p2tp")
                mk_tp = B.sems["p2tp"].mark()
                B.w(sc, mk_tp)
                mk_ev = B.i(sc, SC.copy(xqfm[p][:].rearrange("p (k t) -> p k t", t=128)[:, k4:k4 + 4, :], bank[:].rearrange("p (k t) -> p k t", t=128)), "p2ev")
                B.bank_guard[bidx] = mk_ev
            mk_xq_used[p] = B.sems["p2tp"].mark()
            mk_xqfm = B.sems["p2ev"].mark()
            B.w(gp, mk_xqfm)
            mk_xqs[p] = B.i(gp, GP.dma_start(out=xq_s[c, :, :], in_=xqfm[p][:]), f"xqs{p}", 16)

            B.w(te, mk_htm)
            for k4 in range(0, KT, 4):
                bidx = 4 + (k4 // 4) % 4
                bank = pb[bidx]
                B.guard_bank(te, bidx)
                for kk in range(4):
                    k = k4 + kk
                    B.i(te, TE.matmul(bank[:, 128 * kk:128 * (kk + 1)], htm[p][:, 128 * k:128 * (k + 1)], ident[:], start=True, stop=True), "p2tp")
                mk_tp = B.sems["p2tp"].mark()
                B.w(sc, mk_tp)
                dst = hfm[:].rearrange("p (k t) -> p k t", t=NT)[:, k4:k4 + 4, CH * c:CH * (c + 1)]
                mk_ev = B.i(sc, SC.copy(dst, bank[:].rearrange("p (k t) -> p k t", t=128)), "p2ev")
                B.bank_guard[bidx] = mk_ev
            mk_htm_used[p] = B.sems["p2tp"].mark()
            mk_hfm[c] = B.sems["p2ev"].mark()
            if h_sdbg is not None:
                B.w(gp, mk_htm)
                mk_hdbg = B.i(gp, GP.dma_start(out=h_sdbg[c, :, :], in_=htm[p][:]), "dbgs", 16)
                mk_htm_used[p] = (mk_htm_used[p], mk_hdbg)

        pend_tp.append(do_transposes)

    while pend_tp:
        pend_tp.pop(0)()
    pump_qw(NWQ)
    mk_qar = part_sum(ACC_Q, TMP1, mk_qwabs[NWQ - 1])
    B.w(ve, mk_qar)
    B.f(ve, VE.tensor_scalar(TMP1, TMP1, RC3M, None, Alu.mult), "qkq")
    B.f(ve, VE.tensor_scalar(TMP1, TMP1, 1e-5, None, Alu.max), "qkq")
    B.f(ve, VE.tensor_scalar(THRQ, TMP1, 0.5, None, Alu.mult), "qkq")
    B.f(ve, VE.tensor_scalar(NTHRQ, TMP1, -0.5, None, Alu.mult), "qkq")
    B.f(ve, VE.reciprocal(TMP0, TMP1), "qkq")
    B.f(ve, VE.reciprocal(MAGQ, TMP0), "qkq")
    mk_qthr = vpub()

    if stop_after == "P2a":
        B.barrier([gp])
        B.audit()
        return nc, ctx, None
    B.barrier([sy, sc, ve, te, gp])
    st2.close()

    st3 = contextlib.ExitStack()
    ctx.enter_context(st3)
    wbuf = [sb(st3, f"wbuf{i}", [128, DM], F32) for i in range(2)]
    qc1 = sb(st3, "qc1", [128, DM], BF16)
    qc2 = sb(st3, "qc2", [128, DM], BF16)
    tern = [sb(st3, f"tern{i}", [128, DM], BF16) for i in range(2)]
    wqt = [sb(st3, f"wqt{i}", [128, DM], BF16) for i in range(2)]
    wvt = sb(st3, "wvt", [128, KT * 512], BF16)
    qev = [sb(st3, f"qev{i}", [128, 512], BF16) for i in range(2)]
    fwb = sb(st3, "fwb", [128, DM], F32)

    mk_tern = {}
    tern_free = {0: None, 1: None}
    wbuf_free = {0: None, 1: None}

    def quant_stream(j, src, thr_ap, nthr_ap, extra_v_wait=None):
        pbuf = j % 2
        B.w(sy, wbuf_free[pbuf])
        mk_ld = B.i(sy, SY.dma_start(out=wbuf[pbuf][:], in_=src), f"p3wld{pbuf}", 16)
        B.w(ve, mk_ld)
        if extra_v_wait is not None:
            B.w(ve, extra_v_wait)
        B.w(ve, tern_free[pbuf])
        B.f(ve, VE.tensor_scalar(qc1[:], wbuf[pbuf][:], thr_ap, None, Alu.is_gt), "p3q")
        mk_c2 = B.f(ve, VE.tensor_scalar(qc2[:], wbuf[pbuf][:], nthr_ap, None, Alu.is_lt), "p3q")
        wbuf_free[pbuf] = mk_c2
        mk_tern[j] = B.f(ve, VE.tensor_tensor(tern[pbuf][:], qc1[:], qc2[:], Alu.subtract), "p3q")
        return mk_tern[j]

    def transpose_16(src_bf16, dst_view_fn, evsem, b0=6, nb=2):
        for k4 in range(0, KT, 4):
            bidx = b0 + (k4 // 4) % nb
            bank = pb[bidx]
            B.guard_bank(te, bidx)
            for kk in range(4):
                k = k4 + kk
                B.i(te, TE.matmul(bank[:, 128 * kk:128 * (kk + 1)], src_bf16[:, 128 * k:128 * (k + 1)], ident[:], start=True, stop=True), "p3tp")
            mk_tp = B.sems["p3tp"].mark()
            B.w(sc, mk_tp)
            mk_ev = B.i(sc, SC.copy(dst_view_fn(k4), bank[:].rearrange("p (k t) -> p k t", t=128)), evsem)
            B.bank_guard[bidx] = mk_ev
        return B.sems["p3tp"].mark(), B.sems[evsem].mark()

    fw_next = [0]
    mk_fwabs = [None] * KT

    def pump_fw(kmax):
        while fw_next[0] < min(kmax, KT):
            j = fw_next[0]
            if j > 0:
                B.w(gp, mk_fwabs[j - 1])
            mk_ld = B.i(gp, GP.dma_start(out=fwb[:], in_=ffnw_d[128 * j:128 * (j + 1), :]), "fwld", 16)
            B.w(ve, mk_ld)
            B.f(ve, VE.reduce_sum(TMP0, fwb[:], axis=AX, apply_absolute_value=True), "fwabs")
            if j == 0:
                mk_fwabs[j] = B.f(ve, VE.tensor_copy(ACC_F, TMP0), "fwabs")
            else:
                mk_fwabs[j] = B.f(ve, VE.tensor_tensor(ACC_F, ACC_F, TMP0, Alu.add), "fwabs")
            fw_next[0] += 1

    mk_qk_used = {}
    mk_qev_free = {0: None, 1: None}
    mk_wqt = {}

    def prep_qk(f):
        quant_stream(f, qkvw_d[128 * f:128 * (f + 1), :], THRQ, NTHRQ,
                     extra_v_wait=(mk_qthr if f == 0 else None))
        B.w(te, mk_tern[f])
        wq = wqt[f % 2]
        if f >= 2:
            B.w(sc, mk_qk_used[f - 2])
        _, mk_wqt[f] = transpose_16(tern[f % 2][:], lambda k4: wq[:].rearrange("p (k t) -> p k t", t=128)[:, k4:k4 + 4, :], "p3wev", b0=0)
        tern_free[f % 2] = B.sems["p3tp"].mark()

    def mm_qk(f):
        wq = wqt[f % 2]
        B.w(te, mk_wqt[f])
        if f == 0:
            B.w(te, mk_hfm[nch - 1])
        for g in range(NG):
            bidx = 4 + g % 2
            bank = pb[bidx]
            B.guard_bank(te, bidx)
            for k in range(KT):
                B.i(te, TE.matmul(bank[:, 0:GSZ], wq[:, 128 * k:128 * (k + 1)], hfm[:, NT * k + GSZ * g: NT * k + GSZ * (g + 1)], start=(k == 0), stop=(k == KT - 1)), "p3mm")
            mk_mm = B.sems["p3mm"].mark()
            if g == NG - 1:
                mk_qk_used[f] = mk_mm
            B.w(sc, mk_mm)
            B.w(sc, mk_qev_free[g % 2])
            if f == 0 and g == 0:
                B.w(sc, mk_qthr)
            mk_e = B.i(sc, SC.mul(qev[g % 2][:, 0:GSZ], bank[:, 0:GSZ], MAGQ), "p3qev")
            B.bank_guard[bidx] = mk_e
            B.w(gp, mk_e)
            mk_qev_free[g % 2] = B.i(gp, GP.dma_start(out=qk_s[f, :, GSZ * g:GSZ * (g + 1)], in_=qev[g % 2][:, 0:GSZ]), f"p3qwr{g % 2}", 16)

    prep_qk(0)
    for f in range(2 * NH):
        if f + 1 < 2 * NH:
            prep_qk(f + 1)
        mm_qk(f)
        pump_fw((f + 1) // 2)

    mk_vev_free = dict(mk_qev_free)
    for vg in range(4):
        for fb in range(4):
            f = 2 * NH + 4 * vg + fb
            quant_stream(f, qkvw_d[128 * f:128 * (f + 1), :], THRQ, NTHRQ)
            B.w(te, mk_tern[f])
            if fb == 0 and vg > 0:
                B.w(sc, mk_v_used)
            _, mk_wvte = transpose_16(
                tern[f % 2][:],
                lambda k4: wvt[:].rearrange("p (k t) -> p k t", t=512)[:, k4:k4 + 4, 128 * fb:128 * (fb + 1)],
                "p3wev", b0=0)
            tern_free[f % 2] = B.sems["p3tp"].mark()
        B.w(te, mk_wvte)
        for c in range(nch):
            bidx = 4 + c % 2
            bank = pb[bidx]
            B.guard_bank(te, bidx)
            for k in range(KT):
                B.i(te, TE.matmul(bank[:, 0:512], hfm[:, NT * k + CH * c: NT * k + CH * (c + 1)], wvt[:, 512 * k:512 * (k + 1)], start=(k == 0), stop=(k == KT - 1)), "p3mm")
            mk_mm = B.sems["p3mm"].mark()
            if c == nch - 1:
                mk_v_used = mk_mm
            B.w(sc, mk_mm)
            B.w(sc, mk_vev_free[c % 2])
            mk_e = B.i(sc, SC.mul(qev[c % 2][:, 0:512], bank[:, 0:512], MAGQ), "p3qev")
            B.bank_guard[bidx] = mk_e
            B.w(gp, mk_e)
            mk_vev_free[c % 2] = B.i(gp, GP.dma_start(out=v_s[vg, c, :, :], in_=qev[c % 2][:, 0:512]), f"p3qwr{c % 2}", 16)
        pump_fw(8 + 2 * (vg + 1))

    mk_pwabs = [None] * KT
    for j in range(KT):
        B.w(sy, mk_pwabs[j - 2] if j >= 2 else wbuf_free[j % 2])
        mk_ld = B.i(sy, SY.dma_start(out=wbuf[j % 2][:], in_=projw_d[128 * j:128 * (j + 1), :]), f"p3pld{j % 2}", 16)
        B.w(ve, mk_ld)
        B.f(ve, VE.reduce_sum(TMP0, wbuf[j % 2][:], axis=AX, apply_absolute_value=True), "ppabs")
        if j == 0:
            mk_pwabs[j] = B.f(ve, VE.tensor_copy(ACC_P, TMP0), "ppabs")
        else:
            mk_pwabs[j] = B.f(ve, VE.tensor_tensor(ACC_P, ACC_P, TMP0, Alu.add), "ppabs")
        wbuf_free[j % 2] = mk_pwabs[j]
    mk_par = part_sum(ACC_P, TMP1, mk_pwabs[KT - 1])
    B.w(ve, mk_par)
    B.f(ve, VE.tensor_scalar(TMP1, TMP1, 1.0 / (DM * DM), None, Alu.mult), "ppq")
    B.f(ve, VE.tensor_scalar(TMP1, TMP1, 1e-5, None, Alu.max), "ppq")
    B.f(ve, VE.tensor_scalar(THRP, TMP1, 0.5, None, Alu.mult), "ppq")
    B.f(ve, VE.tensor_scalar(NTHRP, TMP1, -0.5, None, Alu.mult), "ppq")
    B.f(ve, VE.reciprocal(TMP0, TMP1), "ppq")
    B.f(ve, VE.reciprocal(MAGP, TMP0), "ppq")
    mk_pthr = vpub()
    if stop_after == "P3":
        B.barrier([gp])
        B.audit()
        return nc, ctx, None
    B.barrier([sy, sc, ve, te, gp])
    st3.close()
    hfmst.close()

    wptst = contextlib.ExitStack()
    ctx.enter_context(wptst)
    wpt = sb(wptst, "wpt", [128, KT * DM], BF16)
    st3b = contextlib.ExitStack()
    ctx.enter_context(st3b)
    wbuf = [sb(st3b, f"wbufP{i}", [128, DM], F32) for i in range(2)]
    qc1 = sb(st3b, "qc1P", [128, DM], BF16)
    qc2 = sb(st3b, "qc2P", [128, DM], BF16)
    tern = [sb(st3b, f"ternP{i}", [128, DM], BF16) for i in range(2)]
    mk_tern = {}
    tern_free = {0: None, 1: None}
    wbuf_free = {0: None, 1: None}
    for j in range(KT):
        quant_stream(100 + j, projw_d[128 * j:128 * (j + 1), :], THRP, NTHRP,
                     extra_v_wait=(mk_pthr if j == 0 else None))
        B.w(te, mk_tern[100 + j])
        _, mk_wptev = transpose_16(
            tern[j % 2][:],
            lambda k4: wpt[:].rearrange("p (k t) -> p k t", t=DM)[:, k4:k4 + 4, 128 * j:128 * (j + 1)],
            "p3wev", b0=0)
        tern_free[j % 2] = B.sems["p3tp"].mark()
    mk_wpt = B.sems["p3wev"].mark()
    pump_fw(KT)
    B.barrier([sy, sc, ve, te, gp])
    st3b.close()
    mk_qkv_written = B.sem("p3qwr0").mark()
    mk_qkv_written1 = B.sem("p3qwr1").mark()

    mk_far = part_sum(ACC_F, TMP1, mk_fwabs[KT - 1])
    B.w(ve, mk_far)
    B.f(ve, VE.tensor_scalar(TMP1, TMP1, 1.0 / (DM * DM), None, Alu.mult), "fq")
    B.f(ve, VE.tensor_tensor(WSF, TMP1, EPS_6, Alu.max), "fq")
    B.f(ve, VE.tensor_scalar(THRF, WSF, 0.5, None, Alu.mult), "fq")
    B.f(ve, VE.tensor_scalar(NTHRF, WSF, -0.5, None, Alu.mult), "fq")
    mk_fthr = vpub()


    st2b = contextlib.ExitStack()
    ctx.enter_context(st2b)
    wft = sb(st2b, "wft", [128, KT * DM], BF16)
    wbufB = [sb(st2b, f"wbufB{i}", [128, DM], F32) for i in range(2)]
    qc1B = sb(st2b, "qc1B", [128, DM], BF16)
    qc2B = sb(st2b, "qc2B", [128, DM], BF16)
    ternB = [sb(st2b, f"ternB{i}", [128, DM], BF16) for i in range(2)]
    xqb = [sb(st2b, f"xqb{i}", [128, DM], BF16) for i in range(2)]
    gsb = sb(st2b, "gsb", [128, 1024], F32)

    mk_tB = {}
    mk_tB_used = {}
    mk_c2B = [None, None]
    for j in range(KT):
        pbuf = j % 2
        if mk_c2B[pbuf] is not None:
            B.w(sy, mk_c2B[pbuf])
        mk_ld = B.i(sy, SY.dma_start(out=wbufB[pbuf][:], in_=ffnw_d[128 * j:128 * (j + 1), :]), f"fqld{pbuf}", 16)
        B.w(ve, mk_ld)
        if j == 0:
            B.w(ve, mk_fthr)
        if j - 2 in mk_tB_used:
            B.w(ve, mk_tB_used[j - 2])
        B.f(ve, VE.tensor_scalar(qc1B[:], wbufB[pbuf][:], THRF, None, Alu.is_gt), "fqq")
        mk_c2B[pbuf] = B.f(ve, VE.tensor_scalar(qc2B[:], wbufB[pbuf][:], NTHRF, None, Alu.is_lt), "fqq")
        mk_tB[j] = B.f(ve, VE.tensor_tensor(ternB[pbuf][:], qc1B[:], qc2B[:], Alu.subtract), "fqq")
        B.w(te, mk_tB[j])
        for k4 in range(0, KT, 4):
            bidx = 6 + (k4 // 4) % 2
            bank = pb[bidx]
            B.guard_bank(te, bidx)
            for kk in range(4):
                k = k4 + kk
                B.i(te, TE.matmul(bank[:, 128 * kk:128 * (kk + 1)], ternB[pbuf][:, 128 * k:128 * (k + 1)], ident[:], start=True, stop=True), "fqtp")
            mk_tp = B.sems["fqtp"].mark()
            B.w(sc, mk_tp)
            dst = wft[:].rearrange("p (k t) -> p k t", t=DM)[:, k4:k4 + 4, 128 * j:128 * (j + 1)]
            mk_ev = B.i(sc, SC.copy(dst, bank[:].rearrange("p (k t) -> p k t", t=128)), "fwev")
            B.bank_guard[bidx] = mk_ev
        mk_tB_used[j] = B.sems["fqtp"].mark()
    mk_wft = B.sems["fwev"].mark()

    mk_gw_free = {0: None, 1: None}
    mk_xqld = [None] * nch
    for c in range(nch):
        pbuf = c % 2
        if c >= 2:
            B.w(sy, mk_xq_used[c - 2])
        mk_xqld[c] = B.i(sy, SY.dma_start(out=xqb[pbuf][:], in_=xq_s[c, :, :]), f"xqld{pbuf}", 16)
        B.w(ve, mk_fthr)
        if c > 0:
            B.w(ve, mk_gev)
        B.f(ve, VE.scalar_tensor_tensor(st_f, colw[:, 32 + c:33 + c], R127, WSF, Alu.mult, Alu.mult), "p2bv")
        mk_f = vpub()
        B.w(te, mk_xqld[c])
        if c == 0:
            B.w(te, mk_wft)
        for o in range(4):
            bidx = 4 + o % 2
            bank = pb[bidx]
            B.guard_bank(te, bidx)
            for k in range(KT):
                B.i(te, TE.matmul(bank[:, 0:512], xqb[pbuf][:, 128 * k:128 * (k + 1)], wft[:, DM * k + 512 * o: DM * k + 512 * (o + 1)], start=(k == 0), stop=(k == KT - 1)), "fmm")
            mk_mm = B.sems["fmm"].mark()
            if o == 3:
                if c == 0:
                    mk_xq_used = {}
                mk_xq_used[c] = mk_mm
            B.w(sc, mk_mm)
            B.w(sc, mk_f)
            B.w(sc, mk_gw_free[o % 2])
            mk_gev = B.i(sc, SC.activation(gsb[:, 512 * (o % 2):512 * (o % 2 + 1)], bank[:, 0:512], Act.Silu, scale=st_f), "fgev")
            B.bank_guard[bidx] = mk_gev
            B.w(gp, mk_gev)
            mk_gw_free[o % 2] = B.i(gp, GP.dma_start(out=g_s[c, :, 512 * o:512 * (o + 1)], in_=gsb[:, 512 * (o % 2):512 * (o % 2 + 1)]), f"gwr{o % 2}", 16)
    mk_g_written = B.sem("gwr0").mark()
    mk_g_written1 = B.sem("gwr1").mark()

    if stop_after == "P2b":
        B.barrier([gp])
        B.audit()
        return nc, ctx, None
    B.barrier([sy, sc, ve, te, gp])
    st2b.close()

    st4 = contextlib.ExitStack()
    ctx.enter_context(st4)
    HGS = min(256, NT)
    qkb = sb(st4, "qkb", [128, 2 * NH * HGS], BF16)
    vb = sb(st4, "vb", [128, DM], BF16)
    gb = sb(st4, "gb", [128, DM], F32)
    xb4 = sb(st4, "xb4", [128, DM], F32)
    aob = sb(st4, "aob", [128, DM], F32)
    owb = sb(st4, "owb", [128, DM], F32)
    r2tm = sb(st4, "r2tm", [128, DM], BF16)
    r2fm = sb(st4, "r2fm", [128, DM], BF16)
    sgb = sb(st4, "sgb", [128, DM], F32)
    outb = sb(st4, "outb", [128, DM], F32)
    S_f = sb(st4, "S_f", [128, NH * 128], F32)
    S_b = sb(st4, "S_b", [128, NH * 128], BF16)
    smt = sb(st4, "smt", [128, 512], BF16)
    qlm = sb(st4, "qlm", [128, 512], BF16)
    ktm = sb(st4, "ktm", [128, 512], BF16)
    lamt = sb(st4, "lamt", [128, NH * 128], BF16)
    maskt = sb(st4, "maskt", [128, NH * 128], F32)
    lamrevw = sb(st4, "lamrevw", [128, NH * 128], F32)
    lamTw = sb(st4, "lamTw", [128, NH * 128], F32)
    nwb = sb(st4, "nwb", [128, 2 * DM], F32)
    qk3 = qkb[:].rearrange("p (f x) -> p f x", x=HGS)

    mk_n0 = B.i(sy, SY.dma_start(out=nwb[:, 0:DM], in_=nw_d[:]), "p4c", 16)
    mk_nwb = B.i(sy, SY.dma_start(out=nwb[:, DM:2 * DM], in_=pn_d[:]), "p4c", 16)
    for h in range(NH):
        nsh = svec[:, NH + h:NH + h + 1]
        B.i(sc, SC.activation(maskt[:, 128 * h:128 * (h + 1)], cons[:, 128:256], Act.Exp, scale=nsh), "p4dc")
        B.i(sc, SC.activation(lamt[:, 128 * h:128 * (h + 1)], cons[:, 0:128], Act.Exp, scale=nsh), "p4dc")
    mk_dc = spub()
    B.w(ve, mk_dc)
    B.w(ve, mk_ones)
    for h in range(NH):
        B.f(ve, VE.tensor_tensor(maskt[:, 128 * h:128 * (h + 1)], maskt[:, 128 * h:128 * (h + 1)], cons[:, 256:384], Alu.mult), "p4dcv")
        B.f(ve, VE.tensor_scalar(lamrevw[:, 128 * h:128 * (h + 1)], ones[:], LAMREV(h), None, Alu.mult), "p4dcv")
        B.f(ve, VE.tensor_scalar(lamTw[:, 128 * h:128 * (h + 1)], ones[:], LAMT(h), None, Alu.mult), "p4dcv")
    mk_masks = vpub()

    B.w(sy, mk_qkv_written)
    B.w(sy, mk_qkv_written1)
    B.w(sy, mk_g_written)
    B.w(sy, mk_g_written1)
    mk_out = None
    mk_smt_ev = None
    mk_qlm_ev = None
    mk_ktm_ev = None
    mk_o_ev = None
    mk_Sb = {}
    mk_Su = None
    for c in range(nch):
        hg, hh = (c * CH) // HGS, ((c * CH) % HGS) // CH
        if hh == 0:
            if hg > 0:
                B.w(sy, mk_hg_used)
            for f in range(2 * NH):
                B.i(sy, SY.dma_start(out=qkb[:, HGS * f:HGS * (f + 1)], in_=qk_s[f, :, HGS * hg:HGS * (hg + 1)]), "p4qk", 16)
            mk_qk_all = B.sems["p4qk"].mark()
        if c > 0:
            B.w(sy, mk_vb_used)
        for vg in range(4):
            mk_vld = B.i(sy, SY.dma_start(out=vb[:, 512 * vg:512 * (vg + 1)], in_=v_s[vg, c, :, :]), "p4v", 16)
        if c > 0:
            B.w(sy, mk_gb_used)
        mk_gld = B.i(sy, SY.dma_start(out=gb[:], in_=g_s[c, :, :]), "p4g", 16)
        if c > 0:
            B.w(sy, mk_xb4_used)
        mk_x4 = B.i(sy, SY.dma_start(out=xb4[:], in_=x_d[CH * c:CH * (c + 1), :]), "p4x", 16)

        for g in range(4):
            heads = list(range(4 * g, 4 * g + 4))
            wide = slice(512 * g, 512 * (g + 1))
            B.w(te, mk_qk_all)
            B.guard_bank(te, 3)
            for i, h in enumerate(heads):
                k_t = qk3[:, NH + h, 128 * hh:128 * hh + 128]
                B.i(te, TE.matmul(pb[3][:, 128 * i:128 * (i + 1)], k_t, ident[:], start=(i == 0), stop=(i == 3)), "p4tp")
            mk_tpk = B.sems["p4tp"].mark()
            B.w(ve, mk_tpk)
            if c == 0 and g == 0:
                B.w(ve, mk_lamT)
                B.w(ve, mk_masks)
            mk_ktm_ev = B.f(ve, VE.tensor_tensor(ktm[:], pb[3][:, 0:512], lamrevw[:, wide], Alu.mult), "p4ktm")
            B.bank_guard[3] = mk_ktm_ev
            B.guard_bank(te, 0)
            for i, h in enumerate(heads):
                k_t = qk3[:, NH + h, 128 * hh:128 * hh + 128]
                q_t = qk3[:, h, 128 * hh:128 * hh + 128]
                B.i(te, TE.matmul(pb[0][:, 128 * i:128 * (i + 1)], k_t, q_t, start=(i == 0), stop=(i == 3)), "p4mm")
            mk_sct = B.sems["p4mm"].mark()
            B.w(ve, mk_sct)
            mk_smt_ev = B.f(ve, VE.tensor_tensor(smt[:], pb[0][:, 0:512], maskt[:, wide], Alu.mult), "p4smt")
            B.bank_guard[0] = mk_smt_ev
            mk_qlm_ev = B.f(ve, VE.tensor_tensor(qlm[:], qk3[:, 4 * g:4 * g + 4, 128 * hh:128 * hh + 128], lamt[:, wide], Alu.mult), "p4qlm")
            B.w(te, mk_smt_ev)
            B.w(te, mk_vld)
            B.guard_bank(te, 1)
            if c > 0:
                B.w(te, mk_qlm_ev)
                B.w(te, mk_Sb[g])
            for i, h in enumerate(heads):
                v_t = vb[:, 128 * h:128 * (h + 1)]
                B.i(te, TE.matmul(pb[1][:, 128 * i:128 * (i + 1)], smt[:, 128 * i:128 * (i + 1)], v_t, start=(i == 0), stop=(c == 0 and i == 3)), "p4mm")
                if c > 0:
                    B.i(te, TE.matmul(pb[1][:, 128 * i:128 * (i + 1)], qlm[:, 128 * i:128 * (i + 1)], S_b[:, 128 * h:128 * (h + 1)], start=False, stop=(i == 3)), "p4mm")
            mk_o = B.sems["p4mm"].mark()
            B.w(sc, mk_o)
            if c > 0 and g == 0:
                B.w(sc, mk_ow_ev)
            mk_o_ev = B.i(sc, SC.mul(aob[:, wide], pb[1][:, 0:512], OSC), "p4aev")
            B.bank_guard[1] = mk_o_ev
            B.w(te, mk_ktm_ev)
            B.guard_bank(te, 2)
            for i, h in enumerate(heads):
                v_t = vb[:, 128 * h:128 * (h + 1)]
                B.i(te, TE.matmul(pb[2][:, 128 * i:128 * (i + 1)], ktm[:, 128 * i:128 * (i + 1)], v_t, start=(i == 0), stop=(i == 3)), "p4mm")
            mk_kv = B.sems["p4mm"].mark()
            B.w(ve, mk_kv)
            if c == 0:
                mk_Su = B.f(ve, VE.tensor_copy(S_f[:, wide], pb[2][:, 0:512]), "p4su")
            else:
                B.f(ve, VE.tensor_tensor(S_f[:, wide], S_f[:, wide], lamTw[:, wide], Alu.mult), "p4su")
                mk_Su = B.f(ve, VE.tensor_tensor(S_f[:, wide], S_f[:, wide], pb[2][:, 0:512], Alu.add), "p4su")
            B.bank_guard[2] = mk_Su
            mk_Sb[g] = B.f(ve, VE.tensor_copy(S_b[:, wide], S_f[:, wide]), "p4sb")
        if hh == (HGS // CH) - 1:
            mk_hg_used = B.sems["p4mm"].mark()
        mk_vb_used = B.sems["p4mm"].mark()

        mk_ao = B.sems["p4aev"].mark()
        B.w(sc, mk_ao)
        if c > 0:
            B.w(sc, mk_xb4_used)
        B.f(sc, SC.activation(sgb[:], aob[:], Act.Square, accum_out=st_sum), "p4s")
        B.f(sc, SC.activation(st_t0, st_sum, Act.Ln, scale=1.0 / DM, bias=EPS_6), "p4s")
        B.f(sc, SC.activation(st_r, st_t0, Act.Exp, scale=-0.5), "p4s")
        mk_r1 = spub()
        B.w(ve, mk_r1)
        B.w(ve, mk_nwb)
        if c > 0:
            B.w(ve, mk_ow_used)
        mk_ow_ev = B.f(ve, VE.tensor_tensor(owb[:], aob[:], nwb[:, 0:DM], Alu.mult), "p4ow")
        B.w(sc, mk_ow_ev)
        B.f(sc, SC.activation(sgb[:], owb[:], Act.Square, accum_out=st_ssq), "p4s")
        mk_s2 = spub()
        B.w(ve, mk_s2)
        B.f(ve, VE.tensor_scalar(st_t0, st_ssq, 1.0 / DM, None, Alu.mult), "p4ve")
        B.f(ve, VE.tensor_tensor(st_t1, st_t0, st_r, Alu.mult), "p4ve")
        B.f(ve, VE.tensor_tensor(st_t1, st_t1, st_r, Alu.mult), "p4ve")
        mk_mm2b = vpub()
        B.w(sc, mk_mm2b)
        B.f(sc, SC.activation(st_t0, st_t1, Act.Ln, bias=EPSM_AD), "p4s")
        B.f(sc, SC.activation(st_r2, st_t0, Act.Exp, scale=-0.5), "p4s")
        mk_r2b = spub()
        B.w(ve, mk_r2b)
        B.f(ve, VE.tensor_tensor(st_rr, st_r, st_r2, Alu.mult), "p4ve")
        mk_r12 = vpub()
        B.w(ve, mk_r12)
        if c > 0:
            B.w(ve, mk_r2tm_used)
        B.f(ve, VE.scalar_tensor_tensor(r2tm[:], owb[:], st_rr, nwb[:, DM:2 * DM], Alu.mult, Alu.mult), "p4r2")
        mk_r2tm = vpub()
        mk_ow_used = mk_r2tm

        B.w(te, mk_r2tm)
        if c > 0:
            B.w(sc, mk_r2fm_used)
        for k4 in range(0, KT, 4):
            bidx = 6 + (k4 // 4) % 2
            bank = pb[bidx]
            B.guard_bank(te, bidx)
            for kk in range(4):
                k = k4 + kk
                B.i(te, TE.matmul(bank[:, 128 * kk:128 * (kk + 1)], r2tm[:, 128 * k:128 * (k + 1)], ident[:], start=True, stop=True), "p4tp")
            mk_tp = B.sems["p4tp"].mark()
            B.w(sc, mk_tp)
            mk_ev = B.i(sc, SC.copy(r2fm[:].rearrange("p (k t) -> p k t", t=128)[:, k4:k4 + 4, :], bank[:].rearrange("p (k t) -> p k t", t=128)), "p4rev")
            B.bank_guard[bidx] = mk_ev
        mk_r2tm_used = B.sems["p4tp"].mark()
        mk_r2fm = B.sems["p4rev"].mark()

        B.w(te, mk_r2fm)
        if c == 0:
            B.w(te, mk_wpt)
        for o in range(4):
            bidx = 4 + o % 2
            bank = pb[bidx]
            B.guard_bank(te, bidx)
            for k in range(KT):
                B.i(te, TE.matmul(bank[:, 0:512], r2fm[:, 128 * k:128 * (k + 1)], wpt[:, DM * k + 512 * o: DM * k + 512 * (o + 1)], start=(k == 0), stop=(k == KT - 1)), "p4pm")
            mk_pm = B.sems["p4pm"].mark()
            if o == 3:
                mk_r2fm_used = mk_pm
            B.w(ve, mk_pm)
            B.w(ve, mk_gld)
            mk_sg = B.f(ve, VE.scalar_tensor_tensor(sgb[:, 512 * o:512 * (o + 1)], bank[:, 0:512], MAGP, gb[:, 512 * o:512 * (o + 1)], Alu.mult, Alu.mult), "p4sg")
            B.bank_guard[bidx] = mk_sg
            B.w(ve, mk_x4)
            if c > 0:
                B.w(ve, mk_ow_free[o])
            mk_oc = B.f(ve, VE.tensor_tensor(outb[:, 512 * o:512 * (o + 1)], sgb[:, 512 * o:512 * (o + 1)], xb4[:, 512 * o:512 * (o + 1)], Alu.add), "p4oc")
            B.w(gp, mk_oc)
            mk_out = B.i(gp, GP.dma_start(out=y_d[CH * c:CH * (c + 1), 512 * o:512 * (o + 1)], in_=outb[:, 512 * o:512 * (o + 1)]), f"yout{o}", 16)
            if c == 0 and o == 0:
                mk_ow_free = {}
            mk_ow_free[o] = mk_out
        mk_gb_used = mk_sg
        mk_xb4_used = mk_oc

    B.barrier([gp])
    B.audit()
    return nc, ctx, None


def _consts():
    t = np.arange(128, dtype=np.float32)
    tt = np.tile(t[None, :], (128, 1))
    diff = t[None, :] - t[:, None]
    diffc = np.maximum(diff, 0.0).astype(np.float32)
    lmask = (diff >= 0).astype(np.float32)
    tmin = (128.0 - t)[:, None].astype(np.float32)
    eps = np.zeros((128, 8), np.float32)
    eps[:, 0] = 1e-5
    eps[:, 1] = np.finfo(np.float32).eps
    eps[:, 2] = 1e-8
    eps[:, 3] = 1e-6
    ident = np.eye(128, dtype=ml_dtypes.bfloat16)
    return {"c_ident": ident, "c_tt": tt.astype(np.float32), "c_diffc": diffc,
            "c_lmask": lmask, "c_tmin": tmin, "c_eps": eps}


def kernel(L0_attn_x, L0_attn_qkv_w, L0_attn_qkv_norm_w, L0_attn_proj_w,
           L0_attn_proj_norm_w, L0_attn_norm_w, L0_attn_s, L0_ffn_weight):
    x = np.ascontiguousarray(np.asarray(L0_attn_x, dtype=np.float32))
    nb = x.shape[0]
    shared = {
        "qkv_w": np.ascontiguousarray(np.asarray(L0_attn_qkv_w, np.float32)),
        "proj_w": np.ascontiguousarray(np.asarray(L0_attn_proj_w, np.float32)),
        "ffn_w": np.ascontiguousarray(np.asarray(L0_ffn_weight, np.float32)),
        "qkv_norm_w": np.tile(np.asarray(L0_attn_qkv_norm_w, np.float32).reshape(1, DM), (128, 1)),
        "proj_norm_w": np.tile(np.asarray(L0_attn_proj_norm_w, np.float32).reshape(1, DM), (128, 1)),
        "norm_w": np.tile(np.asarray(L0_attn_norm_w, np.float32).reshape(1, DM), (128, 1)),
        "s_dec": np.tile(np.asarray(L0_attn_s, np.float32).reshape(1, NH), (128, 1)),
        **_consts(),
    }
    nc, ctx, _ = build_program(nch=x.shape[1] // CH)
    with ctx:
        in_maps = [dict(shared, x=np.ascontiguousarray(x[b])) for b in range(nb)]
        res = run_bass_kernel_spmd(nc, in_maps, core_ids=list(range(nb)))
    return np.stack([np.asarray(r["y"], np.float32) for r in res.results], axis=0)


# version 12
# speedup vs baseline: 1.1152x; 1.0602x over previous
import contextlib
import numpy as np
import ml_dtypes
import concourse.bass as bass
import concourse.mybir as mybir
from concourse.bass_utils import run_bass_kernel_spmd
from concourse.bass_isa import ReduceOp

F32 = mybir.dt.float32
BF16 = mybir.dt.bfloat16
I32 = mybir.dt.int32
Alu = mybir.AluOpType
Act = mybir.ActivationFunctionType
AX = mybir.AxisListType.X

DM = 2048
NH = 16
CH = 128
KT = DM // 128
SCALE_O = float(128) ** -0.5


class SemT:
    def __init__(self, bld, name):
        self.name = name
        self.h = bld.ctx.enter_context(bld.nc.semaphore(name))
        self.v = 0

    def mark(self):
        return (self, self.v)


class Eng:
    def __init__(self, name, h):
        self.name = name
        self.h = h


class Bld:
    def __init__(self, nc, ctx):
        self.nc = nc
        self.ctx = ctx
        self.sems = {}
        self.log = {e: [] for e in ("sync", "scalar", "vector", "tensor", "gpsimd")}
        self.bank_guard = {}

    def sem(self, name):
        if name not in self.sems:
            self.sems[name] = SemT(self, name)
        return self.sems[name]

    def w(self, eng, mark):
        if mark is None:
            return
        if isinstance(mark, tuple) and len(mark) == 2 and isinstance(mark[0], tuple):
            self.w(eng, mark[0]); self.w(eng, mark[1]); return
        sem, val = mark
        if val <= 0:
            return
        eng.h.wait_ge(sem.h, val)
        self.log[eng.name].append(("wait", sem.name, val))

    def i(self, eng, instr, semname, n=1):
        sem = self.sem(semname)
        instr.then_inc(sem.h, n)
        sem.v += n
        self.log[eng.name].append(("inc", sem.name, n))
        return sem.mark()

    def f(self, eng, instr, semname):
        mk = self.i(eng, instr, semname)
        self.w(eng, mk)
        return mk

    def guard_bank(self, te, idx):
        self.w(te, self.bank_guard.get(idx))

    def barrier(self, engs):
        marks = [(s, s.v) for s in list(self.sems.values())]
        for e in engs:
            for mk in marks:
                self.w(e, mk)

    def audit(self):
        ptr = {e: 0 for e in self.log}
        vals = {}
        progress = True
        while progress:
            progress = False
            for e, evs in self.log.items():
                while ptr[e] < len(evs):
                    kind, s, n = evs[ptr[e]]
                    if kind == "wait" and vals.get(s, 0) < n:
                        break
                    if kind == "inc":
                        vals[s] = vals.get(s, 0) + n
                    ptr[e] += 1
                    progress = True
        stuck = {e: (ptr[e], len(evs), evs[ptr[e]]) for e, evs in self.log.items()
                 if ptr[e] < len(evs)}
        if stuck:
            raise RuntimeError(f"deadlock audit failed: {stuck}")


def build_program(nch=16, stop_after="P4", dbg_scratch=False):
    NT = nch * CH
    GSZ = 512 if NT >= 512 else NT
    NG = NT // GSZ
    nc = bass.Bass("TRN2", target_bir_lowering=False)
    ctx = contextlib.ExitStack()
    B = Bld(nc, ctx)
    sy = Eng("sync", nc.sync)
    sc = Eng("scalar", nc.scalar)
    ve = Eng("vector", nc.vector)
    te = Eng("tensor", nc.tensor)
    gp = Eng("gpsimd", nc.gpsimd)
    SY, SC, VE, TE, GP = nc.sync, nc.scalar, nc.vector, nc.tensor, nc.gpsimd

    x_d = nc.dram_tensor("x", [NT, DM], F32, kind="ExternalInput").ap()
    qkvw_d = nc.dram_tensor("qkv_w", [3 * DM, DM], F32, kind="ExternalInput").ap()
    projw_d = nc.dram_tensor("proj_w", [DM, DM], F32, kind="ExternalInput").ap()
    ffnw_d = nc.dram_tensor("ffn_w", [DM, DM], F32, kind="ExternalInput").ap()
    qn_d = nc.dram_tensor("qkv_norm_w", [128, DM], F32, kind="ExternalInput").ap()
    pn_d = nc.dram_tensor("proj_norm_w", [128, DM], F32, kind="ExternalInput").ap()
    nw_d = nc.dram_tensor("norm_w", [128, DM], F32, kind="ExternalInput").ap()
    s_d = nc.dram_tensor("s_dec", [128, NH], F32, kind="ExternalInput").ap()
    ident_d = nc.dram_tensor("c_ident", [128, 128], BF16, kind="ExternalInput").ap()
    tt_d = nc.dram_tensor("c_tt", [128, 128], F32, kind="ExternalInput").ap()
    diffc_d = nc.dram_tensor("c_diffc", [128, 128], F32, kind="ExternalInput").ap()
    lmask_d = nc.dram_tensor("c_lmask", [128, 128], F32, kind="ExternalInput").ap()
    tmin_d = nc.dram_tensor("c_tmin", [128, 1], F32, kind="ExternalInput").ap()
    eps_d = nc.dram_tensor("c_eps", [128, 8], F32, kind="ExternalInput").ap()
    y_d = nc.dram_tensor("y", [NT, DM], F32, kind="ExternalOutput").ap()
    skind = dict(kind="ExternalOutput") if dbg_scratch else {}
    g_s = nc.dram_tensor("g_s", [nch, 128, DM], F32, **skind).ap()
    qk_s = nc.dram_tensor("qk_s", [2 * NH, 128, NT], BF16, **skind).ap()
    v_s = nc.dram_tensor("v_s", [4, nch, 128, 512], BF16, **skind).ap()
    xq_s = nc.dram_tensor("xq_s", [nch, 128, DM], BF16, **skind).ap()
    h_sdbg = nc.dram_tensor("h_sdbg", [nch, 128, DM], BF16, kind="ExternalOutput").ap() if dbg_scratch else None
    ao_dbg = nc.dram_tensor("ao_dbg", [128, DM], F32, kind="ExternalOutput").ap() if dbg_scratch else None
    cw_dbg = nc.dram_tensor("cw_dbg", [128, 64], F32, kind="ExternalOutput").ap() if dbg_scratch else None
    r2_dbg = nc.dram_tensor("r2_dbg", [128, DM], BF16, kind="ExternalOutput").ap() if dbg_scratch else None
    sg_dbg = nc.dram_tensor("sg_dbg", [128, DM], F32, kind="ExternalOutput").ap() if dbg_scratch else None
    xn_dbg = nc.dram_tensor("xn_dbg", [128, DM], F32, kind="ExternalOutput").ap() if dbg_scratch else None
    z_dbg = nc.dram_tensor("z_dbg", [128, DM], F32, kind="ExternalOutput").ap() if dbg_scratch else None
    st_dbg = nc.dram_tensor("st_dbg", [128, 16], F32, kind="ExternalOutput").ap() if dbg_scratch else None

    sb = lambda st, name, shp, dt: st.enter_context(nc.sbuf_tensor(name, shp, dt))

    ident = sb(ctx, "ident", [128, 128], BF16)
    cons = sb(ctx, "cons", [128, 384], F32)
    tmin = sb(ctx, "tmin", [128, 1], F32)
    eps = sb(ctx, "eps", [128, 8], F32)
    svec = sb(ctx, "svec", [128, NH * 4], F32)
    colw = sb(ctx, "colw", [128, 64], F32)
    stats = sb(ctx, "stats", [128, 16], F32)
    pb = [ctx.enter_context(nc.psum_tensor(f"pb{i}", [128, 512], F32)) for i in range(8)]
    ones = sb(ctx, "ones", [128, 128], F32)
    hfmst = contextlib.ExitStack()
    ctx.enter_context(hfmst)
    hfm = sb(hfmst, "hfm", [128, KT * NT], BF16)

    EPS_LN = eps[:, 0:1]
    EPS_M = eps[:, 1:2]
    EPS_8 = eps[:, 2:3]
    EPS_6 = eps[:, 3:4]
    THRQ, MAGQ = colw[:, 16:17], colw[:, 17:18]
    NTHRQ, NTHRP, NTHRF = colw[:, 27:28], colw[:, 28:29], colw[:, 29:30]
    THRP, MAGP = colw[:, 18:19], colw[:, 19:20]
    THRF, WSF = colw[:, 20:21], colw[:, 21:22]
    ACC_Q = colw[:, 22:23]
    ACC_F = colw[:, 23:24]
    ACC_P = colw[:, 24:25]
    TMP0, TMP1 = colw[:, 25:26], colw[:, 26:27]

    TRASH0, TRASH1 = stats[:, 14:15], stats[:, 15:16]

    def vpub():
        return B.f(ve, VE.tensor_copy(TRASH0, TRASH1), "pubv")

    def spub():
        return B.f(sc, SC.copy(colw[:, 56:57], colw[:, 57:58]), "pubs")

    st_sum, st_m, st_r, st_b, st_v = (stats[:, i:i + 1] for i in range(5))
    st_a, st_q, st_r2, st_rr, st_brr = (stats[:, i:i + 1] for i in range(5, 10))
    st_f, st_t0, st_t1, st_ssq = (stats[:, i:i + 1] for i in range(10, 14))

    B.i(sy, SY.dma_start(out=ident[:], in_=ident_d[:]), "ld0", 16)
    B.i(sy, SY.dma_start(out=cons[:, 0:128], in_=tt_d[:]), "ld0", 16)
    B.i(sy, SY.dma_start(out=cons[:, 128:256], in_=diffc_d[:]), "ld0", 16)
    B.i(sy, SY.dma_start(out=cons[:, 256:384], in_=lmask_d[:]), "ld0", 16)
    B.i(sy, SY.dma_start(out=tmin[:], in_=tmin_d[:]), "ld0", 16)
    B.i(sy, SY.dma_start(out=eps[:], in_=eps_d[:]), "ld0", 16)
    mk_ld0 = B.i(sy, SY.dma_start(out=svec[:, 0:NH], in_=s_d[:]), "ld0", 16)

    B.i(ve, VE.memset(stats[:, 14:16], 0.0), "p0v")
    B.i(ve, VE.memset(colw[:, 56:58], 0.0), "p0v")
    B.f(ve, VE.memset(colw[:, 48:49], float(3 * DM * DM)), "p0v")
    B.f(ve, VE.reciprocal(colw[:, 49:50], colw[:, 48:49]), "p0v")
    B.f(ve, VE.memset(colw[:, 50:51], 127.0), "p0v")
    B.f(ve, VE.reciprocal(colw[:, 51:52], colw[:, 50:51]), "p0v")
    RC3M = colw[:, 49:50]
    R127 = colw[:, 51:52]
    OSC = colw[:, 53:54]
    EPSM_AD = colw[:, 55:56]
    B.f(ve, VE.memset(colw[:, 52:53], 128.0), "p0v")
    B.f(ve, VE.memset(colw[:, 54:55], 3.0), "p0v")
    B.f(ve, VE.reciprocal(EPSM_AD, colw[:, 54:55]), "p0v")
    B.f(ve, VE.tensor_scalar(EPSM_AD, EPSM_AD, 4.0, -1.0, Alu.mult, Alu.add), "p0v")
    B.f(ve, VE.tensor_scalar(EPSM_AD, EPSM_AD, 3.0, -1.0, Alu.mult, Alu.add), "p0v")
    B.f(ve, VE.tensor_scalar(colw[:, 54:55], EPSM_AD, -1.0, None, Alu.mult), "p0v")
    B.f(ve, VE.tensor_tensor(EPSM_AD, EPSM_AD, colw[:, 54:55], Alu.max), "p0v")
    B.w(ve, mk_ld0)
    mk_ones = B.i(ve, VE.memset(ones[:], 1.0), "p0v")
    B.f(ve, VE.tensor_scalar(svec[:, NH:2 * NH], svec[:, 0:NH], -1.0, None, Alu.mult), "p0v")
    B.f(ve, VE.tensor_scalar(svec[:, 2 * NH:3 * NH], svec[:, NH:2 * NH], tmin[:, 0:1], None, Alu.mult), "p0v")
    mk_lr0 = vpub()
    B.w(sc, mk_lr0)
    B.f(sc, SC.activation(svec[:, 2 * NH:3 * NH], svec[:, 2 * NH:3 * NH], Act.Exp), "p0s")
    B.f(sc, SC.activation(svec[:, 3 * NH:4 * NH], svec[:, 0:NH], Act.Exp, scale=-128.0), "p0s")
    B.f(sc, SC.activation(OSC, colw[:, 52:53], Act.Ln), "p0s")
    B.f(sc, SC.activation(OSC, OSC, Act.Exp, scale=-0.5), "p0s")
    mk_lamT = spub()
    LAMREV = lambda h: svec[:, 2 * NH + h:2 * NH + h + 1]
    LAMT = lambda h: svec[:, 3 * NH + h:3 * NH + h + 1]

    if stop_after == "P0":
        B.w(ve, mk_lamT)
        B.barrier([gp])
        B.audit()
        return nc, ctx, None

    st2 = contextlib.ExitStack()
    ctx.enter_context(st2)
    xb = [sb(st2, f"xb{i}", [128, DM], F32) for i in range(2)]
    dbufs = [sb(st2, f"dbuf{i}", [128, DM], F32) for i in range(2)]
    zbufs = [sb(st2, f"zbuf{i}", [128, DM], F32) for i in range(2)]
    uibs = [sb(st2, f"uib{i}", [128, DM], I32) for i in range(2)]
    dmp = sb(st2, "dmp", [128, DM], BF16)
    cc1 = sb(st2, "cc1", [128, DM], BF16)
    cc2 = sb(st2, "cc2", [128, DM], BF16)
    xqtm = [sb(st2, f"xqtm{i}", [128, DM], BF16) for i in range(2)]
    htm = [sb(st2, f"htm{i}", [128, DM], BF16) for i in range(2)]
    xqfm = [sb(st2, f"xqfm{i}", [128, DM], BF16) for i in range(2)]
    qwb = [sb(st2, f"qwb{i}", [128, DM], F32) for i in range(2)]
    qnb = sb(st2, "qnb", [128, DM], F32)
    p2stats = sb(st2, "p2stats", [128, 24], F32)

    mk_qnb = B.i(sy, SY.dma_start(out=qnb[:], in_=qn_d[:]), "qnbld", 16)

    def part_sum(acc_ap, out_ap, wait_mark):
        B.w(te, wait_mark)
        B.w(te, mk_ones)
        B.guard_bank(te, 6)
        mk = B.i(te, TE.matmul(pb[6][:, 0:1], ones[:], acc_ap, start=True, stop=True), "psum_mm")
        B.w(sc, mk)
        mk_ev = B.f(sc, SC.copy(out_ap, pb[6][:, 0:1]), "psum_ev")
        B.bank_guard[6] = mk_ev
        return spub()

    NWQ = 3 * KT
    mk_qwld = [None] * NWQ
    mk_qwabs = [None] * NWQ
    qw_next = [0]

    def pump_qw(kmax):
        while qw_next[0] < min(kmax, NWQ):
            j = qw_next[0]
            if j > 1:
                B.w(gp, mk_qwabs[j - 2])
            mk_qwld[j] = B.i(gp, GP.dma_start(out=qwb[j % 2][:], in_=qkvw_d[128 * j:128 * (j + 1), :]), f"qwld{j % 2}", 16)
            B.w(ve, mk_qwld[j])
            B.f(ve, VE.reduce_sum(TMP0, qwb[j % 2][:], axis=AX, apply_absolute_value=True), "qwabs")
            if j == 0:
                mk_qwabs[j] = B.f(ve, VE.tensor_copy(ACC_Q, TMP0), "qwabs")
            else:
                mk_qwabs[j] = B.f(ve, VE.tensor_tensor(ACC_Q, ACC_Q, TMP0, Alu.add), "qwabs")
            qw_next[0] += 1

    mk_xld = [None] * nch
    mk_xfree = [None] * nch
    mk_hfm = [None] * nch
    mk_xqs = {0: None, 1: None}
    mk_xq_used = {0: None, 1: None}
    mk_htm_used = {0: None, 1: None}
    pend_tp = []
    for c in range(nch):
        p = c % 2
        xbp, dbp, zbp, uibp = xb[p], dbufs[p], zbufs[p], uibs[p]
        SS = lambda i: p2stats[:, 12 * p + i:12 * p + i + 1]
        SA, SB, M, NB, T, V, U, RR, AQ, QP, RL = (SS(i) for i in range(11))
        if c >= 2:
            B.w(sy, mk_xfree[c - 2])
        mk_xld[c] = B.i(sy, SY.dma_start(out=xbp[:], in_=x_d[CH * c:CH * (c + 1), :]), f"xld{p}", 16)
        pump_qw(3 * (c + 1))

        B.w(sc, mk_xld[c])
        B.f(sc, SC.activation(dmp[:], xbp[:], Act.Copy, accum_out=SA), "p2s")
        B.f(sc, SC.activation(dmp[:], xbp[:], Act.Square, accum_out=SB), "p2s")
        mk_acc = spub()
        if pend_tp:
            pend_tp.pop(0)()

        B.w(ve, mk_acc)
        B.f(ve, VE.tensor_scalar(M, SA, 1.0 / DM, None, Alu.mult), "p2v")
        B.f(ve, VE.tensor_scalar(NB, M, -1.0, None, Alu.mult), "p2v")
        B.f(ve, VE.tensor_scalar(T, SB, 1.0 / DM, None, Alu.mult), "p2v")
        B.f(ve, VE.tensor_tensor(U, M, M, Alu.mult), "p2v")
        B.f(ve, VE.tensor_tensor(V, T, U, Alu.subtract), "p2v")
        B.f(ve, VE.tensor_tensor(U, V, EPS_LN, Alu.add), "p2v")
        B.f(ve, VE.tensor_scalar(T, U, EPSM_AD, None, Alu.mult), "p2v")
        B.f(ve, VE.tensor_tensor(T, V, T, Alu.add), "p2v")
        mk_arg = vpub()
        B.w(sc, mk_arg)
        B.f(sc, SC.activation(RL, T, Act.Ln), "p2s")
        B.f(sc, SC.activation(RR, RL, Act.Exp, scale=-0.5), "p2s")
        B.f(sc, SC.activation(RL, U, Act.Ln), "p2s")
        B.f(sc, SC.activation(RL, RL, Act.Exp, scale=-0.5), "p2s")
        mk_rr = spub()

        B.w(ve, mk_rr)
        mk_d = B.f(ve, VE.tensor_scalar(dbp[:], xbp[:], NB, None, Alu.add), "p2v")
        B.f(ve, VE.reduce_max(AQ, dbp[:], axis=AX, apply_absolute_value=True), "p2v")
        B.f(ve, VE.tensor_tensor(AQ, AQ, EPS_8, Alu.max), "p2v")
        B.f(ve, VE.tensor_scalar(colw[:, 32 + c:33 + c], AQ, RL, None, Alu.mult), "p2v")
        B.f(ve, VE.reciprocal(T, AQ), "p2v")
        B.f(ve, VE.tensor_scalar(QP, T, 127.0, None, Alu.mult), "p2v")
        B.w(ve, mk_qnb)
        B.w(ve, mk_htm_used[p])
        B.f(ve, VE.scalar_tensor_tensor(htm[p][:], dbp[:], RR, qnb[:], Alu.mult, Alu.mult), "p2v")
        mk_htm = vpub()
        B.w(sc, mk_htm)
        mk_z = B.f(sc, SC.activation(zbp[:], dbp[:], Act.Copy, scale=QP), "p2s")
        mk_zp = spub()

        B.w(ve, mk_zp)
        B.f(ve, VE.tensor_copy(uibp[:], zbp[:]), "p2v")
        B.f(ve, VE.tensor_copy(xbp[:], uibp[:]), "p2v")
        B.f(ve, VE.scalar_tensor_tensor(cc1[:], zbp[:], 0.5, xbp[:], Alu.subtract, Alu.is_gt), "p2v")
        B.f(ve, VE.scalar_tensor_tensor(cc2[:], zbp[:], 0.5, xbp[:], Alu.add, Alu.is_lt), "p2v")
        B.w(ve, mk_xq_used[p])
        B.f(ve, VE.tensor_tensor(xqtm[p][:], xbp[:], cc1[:], Alu.add), "p2v")
        mk_xq = B.f(ve, VE.tensor_tensor(xqtm[p][:], xqtm[p][:], cc2[:], Alu.subtract), "p2v")
        mk_xfree[c] = mk_xq

        def do_transposes(c=c, p=p, mk_xq=mk_xq, mk_htm=mk_htm):
            B.w(te, mk_xq)
            if c == 0:
                B.w(te, mk_ld0)
            B.w(sc, mk_xqs[p])
            for k4 in range(0, KT, 4):
                bidx = (k4 // 4) % 4
                bank = pb[bidx]
                B.guard_bank(te, bidx)
                for kk in range(4):
                    k = k4 + kk
                    B.i(te, TE.matmul(bank[:, 128 * kk:128 * (kk + 1)], xqtm[p][:, 128 * k:128 * (k + 1)], ident[:], start=True, stop=True), "p2tp")
                mk_tp = B.sems["p2tp"].mark()
                B.w(sc, mk_tp)
                mk_ev = B.i(sc, SC.copy(xqfm[p][:].rearrange("p (k t) -> p k t", t=128)[:, k4:k4 + 4, :], bank[:].rearrange("p (k t) -> p k t", t=128)), "p2ev")
                B.bank_guard[bidx] = mk_ev
            mk_xq_used[p] = B.sems["p2tp"].mark()
            mk_xqfm = B.sems["p2ev"].mark()
            B.w(gp, mk_xqfm)
            mk_xqs[p] = B.i(gp, GP.dma_start(out=xq_s[c, :, :], in_=xqfm[p][:]), f"xqs{p}", 16)

            B.w(te, mk_htm)
            for k4 in range(0, KT, 4):
                bidx = 4 + (k4 // 4) % 4
                bank = pb[bidx]
                B.guard_bank(te, bidx)
                for kk in range(4):
                    k = k4 + kk
                    B.i(te, TE.matmul(bank[:, 128 * kk:128 * (kk + 1)], htm[p][:, 128 * k:128 * (k + 1)], ident[:], start=True, stop=True), "p2tp")
                mk_tp = B.sems["p2tp"].mark()
                B.w(sc, mk_tp)
                dst = hfm[:].rearrange("p (k t) -> p k t", t=NT)[:, k4:k4 + 4, CH * c:CH * (c + 1)]
                mk_ev = B.i(sc, SC.copy(dst, bank[:].rearrange("p (k t) -> p k t", t=128)), "p2ev")
                B.bank_guard[bidx] = mk_ev
            mk_htm_used[p] = B.sems["p2tp"].mark()
            mk_hfm[c] = B.sems["p2ev"].mark()
            if h_sdbg is not None:
                B.w(gp, mk_htm)
                mk_hdbg = B.i(gp, GP.dma_start(out=h_sdbg[c, :, :], in_=htm[p][:]), "dbgs", 16)
                mk_htm_used[p] = (mk_htm_used[p], mk_hdbg)

        pend_tp.append(do_transposes)

    while pend_tp:
        pend_tp.pop(0)()
    pump_qw(NWQ)
    mk_qar = part_sum(ACC_Q, TMP1, mk_qwabs[NWQ - 1])
    B.w(ve, mk_qar)
    B.f(ve, VE.tensor_scalar(TMP1, TMP1, RC3M, None, Alu.mult), "qkq")
    B.f(ve, VE.tensor_scalar(TMP1, TMP1, 1e-5, None, Alu.max), "qkq")
    B.f(ve, VE.tensor_scalar(THRQ, TMP1, 0.5, None, Alu.mult), "qkq")
    B.f(ve, VE.tensor_scalar(NTHRQ, TMP1, -0.5, None, Alu.mult), "qkq")
    B.f(ve, VE.reciprocal(TMP0, TMP1), "qkq")
    B.f(ve, VE.reciprocal(MAGQ, TMP0), "qkq")
    mk_qthr = vpub()

    if stop_after == "P2a":
        B.barrier([gp])
        B.audit()
        return nc, ctx, None
    B.barrier([sy, sc, ve, te, gp])
    st2.close()

    st3 = contextlib.ExitStack()
    ctx.enter_context(st3)
    wbuf = [sb(st3, f"wbuf{i}", [128, DM], F32) for i in range(2)]
    qc1 = sb(st3, "qc1", [128, DM], BF16)
    qc2 = sb(st3, "qc2", [128, DM], BF16)
    tern = [sb(st3, f"tern{i}", [128, DM], BF16) for i in range(2)]
    wqt = [sb(st3, f"wqt{i}", [128, DM], BF16) for i in range(2)]
    wvt = sb(st3, "wvt", [128, KT * 512], BF16)
    qev = [sb(st3, f"qev{i}", [128, 512], BF16) for i in range(2)]
    fwb = sb(st3, "fwb", [128, DM], F32)

    mk_tern = {}
    tern_free = {0: None, 1: None}
    wbuf_free = {0: None, 1: None}

    def quant_stream(j, src, thr_ap, nthr_ap, extra_v_wait=None):
        pbuf = j % 2
        B.w(sy, wbuf_free[pbuf])
        mk_ld = B.i(sy, SY.dma_start(out=wbuf[pbuf][:], in_=src), f"p3wld{pbuf}", 16)
        B.w(ve, mk_ld)
        if extra_v_wait is not None:
            B.w(ve, extra_v_wait)
        B.w(ve, tern_free[pbuf])
        B.f(ve, VE.tensor_scalar(qc1[:], wbuf[pbuf][:], thr_ap, None, Alu.is_gt), "p3q")
        mk_c2 = B.f(ve, VE.tensor_scalar(qc2[:], wbuf[pbuf][:], nthr_ap, None, Alu.is_lt), "p3q")
        wbuf_free[pbuf] = mk_c2
        mk_tern[j] = B.f(ve, VE.tensor_tensor(tern[pbuf][:], qc1[:], qc2[:], Alu.subtract), "p3q")
        return mk_tern[j]

    def transpose_16(src_bf16, dst_view_fn, evsem, b0=6, nb=2):
        for k4 in range(0, KT, 4):
            bidx = b0 + (k4 // 4) % nb
            bank = pb[bidx]
            B.guard_bank(te, bidx)
            for kk in range(4):
                k = k4 + kk
                B.i(te, TE.matmul(bank[:, 128 * kk:128 * (kk + 1)], src_bf16[:, 128 * k:128 * (k + 1)], ident[:], start=True, stop=True), "p3tp")
            mk_tp = B.sems["p3tp"].mark()
            B.w(sc, mk_tp)
            mk_ev = B.i(sc, SC.copy(dst_view_fn(k4), bank[:].rearrange("p (k t) -> p k t", t=128)), evsem)
            B.bank_guard[bidx] = mk_ev
        return B.sems["p3tp"].mark(), B.sems[evsem].mark()

    fw_next = [0]
    mk_fwabs = [None] * KT

    def pump_fw(kmax):
        while fw_next[0] < min(kmax, KT):
            j = fw_next[0]
            if j > 0:
                B.w(gp, mk_fwabs[j - 1])
            mk_ld = B.i(gp, GP.dma_start(out=fwb[:], in_=ffnw_d[128 * j:128 * (j + 1), :]), "fwld", 16)
            B.w(ve, mk_ld)
            B.f(ve, VE.reduce_sum(TMP0, fwb[:], axis=AX, apply_absolute_value=True), "fwabs")
            if j == 0:
                mk_fwabs[j] = B.f(ve, VE.tensor_copy(ACC_F, TMP0), "fwabs")
            else:
                mk_fwabs[j] = B.f(ve, VE.tensor_tensor(ACC_F, ACC_F, TMP0, Alu.add), "fwabs")
            fw_next[0] += 1

    mk_qk_used = {}
    mk_qev_free = {0: None, 1: None}
    mk_wqt = {}

    def prep_qk(f):
        quant_stream(f, qkvw_d[128 * f:128 * (f + 1), :], THRQ, NTHRQ,
                     extra_v_wait=(mk_qthr if f == 0 else None))
        B.w(te, mk_tern[f])
        wq = wqt[f % 2]
        if f >= 2:
            B.w(sc, mk_qk_used[f - 2])
        _, mk_wqt[f] = transpose_16(tern[f % 2][:], lambda k4: wq[:].rearrange("p (k t) -> p k t", t=128)[:, k4:k4 + 4, :], "p3wev", b0=0)
        tern_free[f % 2] = B.sems["p3tp"].mark()

    def mm_qk(f):
        wq = wqt[f % 2]
        B.w(te, mk_wqt[f])
        if f == 0:
            B.w(te, mk_hfm[nch - 1])
        for g in range(NG):
            bidx = 4 + g % 2
            bank = pb[bidx]
            B.guard_bank(te, bidx)
            for k in range(KT):
                B.i(te, TE.matmul(bank[:, 0:GSZ], wq[:, 128 * k:128 * (k + 1)], hfm[:, NT * k + GSZ * g: NT * k + GSZ * (g + 1)], start=(k == 0), stop=(k == KT - 1)), "p3mm")
            mk_mm = B.sems["p3mm"].mark()
            if g == NG - 1:
                mk_qk_used[f] = mk_mm
            B.w(sc, mk_mm)
            B.w(sc, mk_qev_free[g % 2])
            if f == 0 and g == 0:
                B.w(sc, mk_qthr)
            mk_e = B.i(sc, SC.mul(qev[g % 2][:, 0:GSZ], bank[:, 0:GSZ], MAGQ), "p3qev")
            B.bank_guard[bidx] = mk_e
            B.w(gp, mk_e)
            mk_qev_free[g % 2] = B.i(gp, GP.dma_start(out=qk_s[f, :, GSZ * g:GSZ * (g + 1)], in_=qev[g % 2][:, 0:GSZ]), f"p3qwr{g % 2}", 16)

    prep_qk(0)
    for f in range(2 * NH):
        if f + 1 < 2 * NH:
            prep_qk(f + 1)
        mm_qk(f)
        pump_fw((f + 1) // 2)

    mk_vev_free = dict(mk_qev_free)
    for vg in range(4):
        for fb in range(4):
            f = 2 * NH + 4 * vg + fb
            quant_stream(f, qkvw_d[128 * f:128 * (f + 1), :], THRQ, NTHRQ)
            B.w(te, mk_tern[f])
            if fb == 0 and vg > 0:
                B.w(sc, mk_v_used)
            _, mk_wvte = transpose_16(
                tern[f % 2][:],
                lambda k4: wvt[:].rearrange("p (k t) -> p k t", t=512)[:, k4:k4 + 4, 128 * fb:128 * (fb + 1)],
                "p3wev", b0=0)
            tern_free[f % 2] = B.sems["p3tp"].mark()
        B.w(te, mk_wvte)
        for c in range(nch):
            bidx = 4 + c % 2
            bank = pb[bidx]
            B.guard_bank(te, bidx)
            for k in range(KT):
                B.i(te, TE.matmul(bank[:, 0:512], hfm[:, NT * k + CH * c: NT * k + CH * (c + 1)], wvt[:, 512 * k:512 * (k + 1)], start=(k == 0), stop=(k == KT - 1)), "p3mm")
            mk_mm = B.sems["p3mm"].mark()
            if c == nch - 1:
                mk_v_used = mk_mm
            B.w(sc, mk_mm)
            B.w(sc, mk_vev_free[c % 2])
            mk_e = B.i(sc, SC.mul(qev[c % 2][:, 0:512], bank[:, 0:512], MAGQ), "p3qev")
            B.bank_guard[bidx] = mk_e
            B.w(gp, mk_e)
            mk_vev_free[c % 2] = B.i(gp, GP.dma_start(out=v_s[vg, c, :, :], in_=qev[c % 2][:, 0:512]), f"p3qwr{c % 2}", 16)
        pump_fw(8 + 2 * (vg + 1))

    mk_pwabs = [None] * KT
    for j in range(KT):
        B.w(sy, mk_pwabs[j - 2] if j >= 2 else wbuf_free[j % 2])
        mk_ld = B.i(sy, SY.dma_start(out=wbuf[j % 2][:], in_=projw_d[128 * j:128 * (j + 1), :]), f"p3pld{j % 2}", 16)
        B.w(ve, mk_ld)
        B.f(ve, VE.reduce_sum(TMP0, wbuf[j % 2][:], axis=AX, apply_absolute_value=True), "ppabs")
        if j == 0:
            mk_pwabs[j] = B.f(ve, VE.tensor_copy(ACC_P, TMP0), "ppabs")
        else:
            mk_pwabs[j] = B.f(ve, VE.tensor_tensor(ACC_P, ACC_P, TMP0, Alu.add), "ppabs")
        wbuf_free[j % 2] = mk_pwabs[j]
    mk_par = part_sum(ACC_P, TMP1, mk_pwabs[KT - 1])
    B.w(ve, mk_par)
    B.f(ve, VE.tensor_scalar(TMP1, TMP1, 1.0 / (DM * DM), None, Alu.mult), "ppq")
    B.f(ve, VE.tensor_scalar(TMP1, TMP1, 1e-5, None, Alu.max), "ppq")
    B.f(ve, VE.tensor_scalar(THRP, TMP1, 0.5, None, Alu.mult), "ppq")
    B.f(ve, VE.tensor_scalar(NTHRP, TMP1, -0.5, None, Alu.mult), "ppq")
    B.f(ve, VE.reciprocal(TMP0, TMP1), "ppq")
    B.f(ve, VE.reciprocal(MAGP, TMP0), "ppq")
    mk_pthr = vpub()
    if stop_after == "P3":
        B.barrier([gp])
        B.audit()
        return nc, ctx, None
    B.barrier([sy, sc, ve, te, gp])
    st3.close()
    hfmst.close()

    wptst = contextlib.ExitStack()
    ctx.enter_context(wptst)
    wpt = sb(wptst, "wpt", [128, KT * DM], BF16)
    st3b = contextlib.ExitStack()
    ctx.enter_context(st3b)
    wbuf = [sb(st3b, f"wbufP{i}", [128, DM], F32) for i in range(2)]
    qc1 = sb(st3b, "qc1P", [128, DM], BF16)
    qc2 = sb(st3b, "qc2P", [128, DM], BF16)
    tern = [sb(st3b, f"ternP{i}", [128, DM], BF16) for i in range(2)]
    mk_tern = {}
    tern_free = {0: None, 1: None}
    wbuf_free = {0: None, 1: None}
    for j in range(KT):
        quant_stream(100 + j, projw_d[128 * j:128 * (j + 1), :], THRP, NTHRP,
                     extra_v_wait=(mk_pthr if j == 0 else None))
        B.w(te, mk_tern[100 + j])
        _, mk_wptev = transpose_16(
            tern[j % 2][:],
            lambda k4: wpt[:].rearrange("p (k t) -> p k t", t=DM)[:, k4:k4 + 4, 128 * j:128 * (j + 1)],
            "p3wev", b0=0)
        tern_free[j % 2] = B.sems["p3tp"].mark()
    mk_wpt = B.sems["p3wev"].mark()
    pump_fw(KT)
    B.barrier([sy, sc, ve, te, gp])
    st3b.close()
    mk_qkv_written = B.sem("p3qwr0").mark()
    mk_qkv_written1 = B.sem("p3qwr1").mark()

    mk_far = part_sum(ACC_F, TMP1, mk_fwabs[KT - 1])
    B.w(ve, mk_far)
    B.f(ve, VE.tensor_scalar(TMP1, TMP1, 1.0 / (DM * DM), None, Alu.mult), "fq")
    B.f(ve, VE.tensor_tensor(WSF, TMP1, EPS_6, Alu.max), "fq")
    B.f(ve, VE.tensor_scalar(THRF, WSF, 0.5, None, Alu.mult), "fq")
    B.f(ve, VE.tensor_scalar(NTHRF, WSF, -0.5, None, Alu.mult), "fq")
    mk_fthr = vpub()


    st2b = contextlib.ExitStack()
    ctx.enter_context(st2b)
    wft = sb(st2b, "wft", [128, KT * DM], BF16)
    wbufB = [sb(st2b, f"wbufB{i}", [128, DM], F32) for i in range(2)]
    qc1B = sb(st2b, "qc1B", [128, DM], BF16)
    qc2B = sb(st2b, "qc2B", [128, DM], BF16)
    ternB = [sb(st2b, f"ternB{i}", [128, DM], BF16) for i in range(2)]
    xqb = [sb(st2b, f"xqb{i}", [128, DM], BF16) for i in range(2)]
    gsb = sb(st2b, "gsb", [128, 1024], F32)

    mk_tB = {}
    mk_tB_used = {}
    mk_c2B = [None, None]
    for j in range(KT):
        pbuf = j % 2
        if mk_c2B[pbuf] is not None:
            B.w(sy, mk_c2B[pbuf])
        mk_ld = B.i(sy, SY.dma_start(out=wbufB[pbuf][:], in_=ffnw_d[128 * j:128 * (j + 1), :]), f"fqld{pbuf}", 16)
        B.w(ve, mk_ld)
        if j == 0:
            B.w(ve, mk_fthr)
        if j - 2 in mk_tB_used:
            B.w(ve, mk_tB_used[j - 2])
        B.f(ve, VE.tensor_scalar(qc1B[:], wbufB[pbuf][:], THRF, None, Alu.is_gt), "fqq")
        mk_c2B[pbuf] = B.f(ve, VE.tensor_scalar(qc2B[:], wbufB[pbuf][:], NTHRF, None, Alu.is_lt), "fqq")
        mk_tB[j] = B.f(ve, VE.tensor_tensor(ternB[pbuf][:], qc1B[:], qc2B[:], Alu.subtract), "fqq")
        B.w(te, mk_tB[j])
        for k4 in range(0, KT, 4):
            bidx = 6 + (k4 // 4) % 2
            bank = pb[bidx]
            B.guard_bank(te, bidx)
            for kk in range(4):
                k = k4 + kk
                B.i(te, TE.matmul(bank[:, 128 * kk:128 * (kk + 1)], ternB[pbuf][:, 128 * k:128 * (k + 1)], ident[:], start=True, stop=True), "fqtp")
            mk_tp = B.sems["fqtp"].mark()
            B.w(sc, mk_tp)
            dst = wft[:].rearrange("p (k t) -> p k t", t=DM)[:, k4:k4 + 4, 128 * j:128 * (j + 1)]
            mk_ev = B.i(sc, SC.copy(dst, bank[:].rearrange("p (k t) -> p k t", t=128)), "fwev")
            B.bank_guard[bidx] = mk_ev
        mk_tB_used[j] = B.sems["fqtp"].mark()
    mk_wft = B.sems["fwev"].mark()

    mk_gw_free = {0: None, 1: None}
    mk_xqld = [None] * nch
    for c in range(nch):
        pbuf = c % 2
        if c >= 2:
            B.w(sy, mk_xq_used[c - 2])
        mk_xqld[c] = B.i(sy, SY.dma_start(out=xqb[pbuf][:], in_=xq_s[c, :, :]), f"xqld{pbuf}", 16)
        B.w(ve, mk_fthr)
        if c > 0:
            B.w(ve, mk_gev)
        B.f(ve, VE.scalar_tensor_tensor(st_f, colw[:, 32 + c:33 + c], R127, WSF, Alu.mult, Alu.mult), "p2bv")
        mk_f = vpub()
        B.w(te, mk_xqld[c])
        if c == 0:
            B.w(te, mk_wft)
        for o in range(4):
            bidx = 4 + o % 2
            bank = pb[bidx]
            B.guard_bank(te, bidx)
            for k in range(KT):
                B.i(te, TE.matmul(bank[:, 0:512], xqb[pbuf][:, 128 * k:128 * (k + 1)], wft[:, DM * k + 512 * o: DM * k + 512 * (o + 1)], start=(k == 0), stop=(k == KT - 1)), "fmm")
            mk_mm = B.sems["fmm"].mark()
            if o == 3:
                if c == 0:
                    mk_xq_used = {}
                mk_xq_used[c] = mk_mm
            B.w(sc, mk_mm)
            B.w(sc, mk_f)
            B.w(sc, mk_gw_free[o % 2])
            mk_gev = B.i(sc, SC.activation(gsb[:, 512 * (o % 2):512 * (o % 2 + 1)], bank[:, 0:512], Act.Silu, scale=st_f), "fgev")
            B.bank_guard[bidx] = mk_gev
            B.w(gp, mk_gev)
            mk_gw_free[o % 2] = B.i(gp, GP.dma_start(out=g_s[c, :, 512 * o:512 * (o + 1)], in_=gsb[:, 512 * (o % 2):512 * (o % 2 + 1)]), f"gwr{o % 2}", 16)
    mk_g_written = B.sem("gwr0").mark()
    mk_g_written1 = B.sem("gwr1").mark()

    if stop_after == "P2b":
        B.barrier([gp])
        B.audit()
        return nc, ctx, None
    B.barrier([sy, sc, ve, te, gp])
    st2b.close()

    st4 = contextlib.ExitStack()
    ctx.enter_context(st4)
    HGS = min(256, NT)
    qkb = sb(st4, "qkb", [128, 2 * NH * HGS], BF16)
    vb = sb(st4, "vb", [128, DM], BF16)
    gb = sb(st4, "gb", [128, DM], F32)
    xb4 = sb(st4, "xb4", [128, DM], F32)
    aob = sb(st4, "aob", [128, DM], F32)
    owb = sb(st4, "owb", [128, DM], F32)
    r2tm = sb(st4, "r2tm", [128, DM], BF16)
    r2fm = sb(st4, "r2fm", [128, DM], BF16)
    sgb = sb(st4, "sgb", [128, DM], F32)
    outb = sb(st4, "outb", [128, DM], F32)
    S_f = sb(st4, "S_f", [128, NH * 128], F32)
    S_b = sb(st4, "S_b", [128, NH * 128], BF16)
    smt = [sb(st4, f"smt{i}", [128, 512], BF16) for i in range(2)]
    qlm = [sb(st4, f"qlm{i}", [128, 512], BF16) for i in range(2)]
    ktm = [sb(st4, f"ktm{i}", [128, 512], BF16) for i in range(2)]
    lamt = sb(st4, "lamt", [128, NH * 128], BF16)
    maskt = sb(st4, "maskt", [128, NH * 128], F32)
    lamrevw = sb(st4, "lamrevw", [128, NH * 128], F32)
    lamTw = sb(st4, "lamTw", [128, NH * 128], F32)
    nwb = sb(st4, "nwb", [128, 2 * DM], F32)
    qk3 = qkb[:].rearrange("p (f x) -> p f x", x=HGS)

    mk_n0 = B.i(sy, SY.dma_start(out=nwb[:, 0:DM], in_=nw_d[:]), "p4c", 16)
    mk_nwb = B.i(sy, SY.dma_start(out=nwb[:, DM:2 * DM], in_=pn_d[:]), "p4c", 16)
    for h in range(NH):
        nsh = svec[:, NH + h:NH + h + 1]
        B.i(sc, SC.activation(maskt[:, 128 * h:128 * (h + 1)], cons[:, 128:256], Act.Exp, scale=nsh), "p4dc")
        B.i(sc, SC.activation(lamt[:, 128 * h:128 * (h + 1)], cons[:, 0:128], Act.Exp, scale=nsh), "p4dc")
    mk_dc = spub()
    B.w(ve, mk_dc)
    B.w(ve, mk_ones)
    for h in range(NH):
        B.f(ve, VE.tensor_tensor(maskt[:, 128 * h:128 * (h + 1)], maskt[:, 128 * h:128 * (h + 1)], cons[:, 256:384], Alu.mult), "p4dcv")
        B.f(ve, VE.tensor_scalar(lamrevw[:, 128 * h:128 * (h + 1)], ones[:], LAMREV(h), None, Alu.mult), "p4dcv")
        B.f(ve, VE.tensor_scalar(lamTw[:, 128 * h:128 * (h + 1)], ones[:], LAMT(h), None, Alu.mult), "p4dcv")
    mk_masks = vpub()

    B.w(sy, mk_qkv_written)
    B.w(sy, mk_qkv_written1)
    B.w(sy, mk_g_written)
    B.w(sy, mk_g_written1)
    mk_out = None
    mk_smt_ev = None
    mk_qlm_ev = None
    mk_ktm_ev = None
    mk_o_ev = None
    mk_Sb = {}
    mk_Su = None
    for c in range(nch):
        hg, hh = (c * CH) // HGS, ((c * CH) % HGS) // CH
        if hh == 0:
            if hg > 0:
                B.w(sy, mk_hg_used)
            for f in range(2 * NH):
                B.i(sy, SY.dma_start(out=qkb[:, HGS * f:HGS * (f + 1)], in_=qk_s[f, :, HGS * hg:HGS * (hg + 1)]), "p4qk", 16)
            mk_qk_all = B.sems["p4qk"].mark()
        if c > 0:
            B.w(sy, mk_vb_used)
        for vg in range(4):
            mk_vld = B.i(sy, SY.dma_start(out=vb[:, 512 * vg:512 * (vg + 1)], in_=v_s[vg, c, :, :]), "p4v", 16)
        if c > 0:
            B.w(sy, mk_gb_used)
        mk_gld = B.i(sy, SY.dma_start(out=gb[:], in_=g_s[c, :, :]), "p4g", 16)
        if c > 0:
            B.w(sy, mk_xb4_used)
        mk_x4 = B.i(sy, SY.dma_start(out=xb4[:], in_=x_d[CH * c:CH * (c + 1), :]), "p4x", 16)

        st2_marks = {}

        def scan_stage1(g, c=c, hh=hh):
            gb2 = g % 2
            heads = list(range(4 * g, 4 * g + 4))
            wide = slice(512 * g, 512 * (g + 1))
            B.w(te, mk_qk_all)
            B.guard_bank(te, 3)
            prev2 = st2_marks.get((c, g - 2)) or st2_marks.get((c - 1, g + 2))
            for i, h in enumerate(heads):
                k_t = qk3[:, NH + h, 128 * hh:128 * hh + 128]
                B.i(te, TE.matmul(pb[3][:, 128 * i:128 * (i + 1)], k_t, ident[:], start=(i == 0), stop=(i == 3)), "p4tp")
            mk_tpk = B.sems["p4tp"].mark()
            B.w(ve, mk_tpk)
            if c == 0 and g == 0:
                B.w(ve, mk_lamT)
                B.w(ve, mk_masks)
            if prev2 is not None:
                B.w(ve, prev2["kv_used"])
                B.w(ve, prev2["o_done"])
            mk_ktm = B.f(ve, VE.tensor_tensor(ktm[gb2][:], pb[3][:, 0:512], lamrevw[:, wide], Alu.mult), "p4ktm")
            B.bank_guard[3] = mk_ktm
            B.guard_bank(te, 0)
            for i, h in enumerate(heads):
                k_t = qk3[:, NH + h, 128 * hh:128 * hh + 128]
                q_t = qk3[:, h, 128 * hh:128 * hh + 128]
                B.i(te, TE.matmul(pb[0][:, 128 * i:128 * (i + 1)], k_t, q_t, start=(i == 0), stop=(i == 3)), "p4mm")
            mk_sct = B.sems["p4mm"].mark()
            B.w(ve, mk_sct)
            mk_smt = B.f(ve, VE.tensor_tensor(smt[gb2][:], pb[0][:, 0:512], maskt[:, wide], Alu.mult), "p4smt")
            B.bank_guard[0] = mk_smt
            mk_qlm = B.f(ve, VE.tensor_tensor(qlm[gb2][:], qk3[:, 4 * g:4 * g + 4, 128 * hh:128 * hh + 128], lamt[:, wide], Alu.mult), "p4qlm")
            return {"g": g, "gb2": gb2, "heads": heads, "wide": wide,
                    "smt": mk_smt, "qlm": mk_qlm, "ktm": mk_ktm}

        def scan_stage2(stg, mk_vld, c=c):
            g, gb2, heads, wide = stg["g"], stg["gb2"], stg["heads"], stg["wide"]
            B.w(te, stg["smt"])
            B.w(te, mk_vld)
            B.guard_bank(te, 1)
            if c > 0:
                B.w(te, stg["qlm"])
                B.w(te, mk_Sb[g])
            for i, h in enumerate(heads):
                v_t = vb[:, 128 * h:128 * (h + 1)]
                B.i(te, TE.matmul(pb[1][:, 128 * i:128 * (i + 1)], smt[gb2][:, 128 * i:128 * (i + 1)], v_t, start=(i == 0), stop=(c == 0 and i == 3)), "p4mm")
                if c > 0:
                    B.i(te, TE.matmul(pb[1][:, 128 * i:128 * (i + 1)], qlm[gb2][:, 128 * i:128 * (i + 1)], S_b[:, 128 * h:128 * (h + 1)], start=False, stop=(i == 3)), "p4mm")
            mk_o = B.sems["p4mm"].mark()
            stg["o_done"] = mk_o
            B.w(sc, mk_o)
            if c > 0 and g == 0:
                B.w(sc, mk_ow_ev)
            mk_o_ev = B.i(sc, SC.mul(aob[:, wide], pb[1][:, 0:512], OSC), "p4aev")
            B.bank_guard[1] = mk_o_ev
            B.w(te, stg["ktm"])
            B.guard_bank(te, 2)
            for i, h in enumerate(heads):
                v_t = vb[:, 128 * h:128 * (h + 1)]
                B.i(te, TE.matmul(pb[2][:, 128 * i:128 * (i + 1)], ktm[gb2][:, 128 * i:128 * (i + 1)], v_t, start=(i == 0), stop=(i == 3)), "p4mm")
            mk_kv = B.sems["p4mm"].mark()
            stg["kv_used"] = mk_kv
            B.w(ve, mk_kv)
            if c == 0:
                B.f(ve, VE.tensor_copy(S_f[:, wide], pb[2][:, 0:512]), "p4su")
                mk_Su = B.sems["p4su"].mark()
            else:
                B.f(ve, VE.tensor_tensor(S_f[:, wide], S_f[:, wide], lamTw[:, wide], Alu.mult), "p4su")
                mk_Su = B.f(ve, VE.tensor_tensor(S_f[:, wide], S_f[:, wide], pb[2][:, 0:512], Alu.add), "p4su")
            B.bank_guard[2] = mk_Su
            mk_Sb[g] = B.f(ve, VE.tensor_copy(S_b[:, wide], S_f[:, wide]), "p4sb")

        pend_st = None
        for g in range(4):
            stg = scan_stage1(g)
            st2_marks[(c, g)] = stg
            if pend_st is not None:
                scan_stage2(pend_st, mk_vld)
            pend_st = stg
        scan_stage2(pend_st, mk_vld)
        if hh == (HGS // CH) - 1:
            mk_hg_used = B.sems["p4mm"].mark()
        mk_vb_used = B.sems["p4mm"].mark()

        mk_ao = B.sems["p4aev"].mark()
        B.w(sc, mk_ao)
        if c > 0:
            B.w(sc, mk_xb4_used)
        B.f(sc, SC.activation(sgb[:], aob[:], Act.Square, accum_out=st_sum), "p4s")
        B.f(sc, SC.activation(st_t0, st_sum, Act.Ln, scale=1.0 / DM, bias=EPS_6), "p4s")
        B.f(sc, SC.activation(st_r, st_t0, Act.Exp, scale=-0.5), "p4s")
        mk_r1 = spub()
        B.w(ve, mk_r1)
        B.w(ve, mk_nwb)
        if c > 0:
            B.w(ve, mk_ow_used)
        mk_ow_ev = B.f(ve, VE.tensor_tensor(owb[:], aob[:], nwb[:, 0:DM], Alu.mult), "p4ow")
        B.w(sc, mk_ow_ev)
        B.f(sc, SC.activation(sgb[:], owb[:], Act.Square, accum_out=st_ssq), "p4s")
        mk_s2 = spub()
        B.w(ve, mk_s2)
        B.f(ve, VE.tensor_scalar(st_t0, st_ssq, 1.0 / DM, None, Alu.mult), "p4ve")
        B.f(ve, VE.tensor_tensor(st_t1, st_t0, st_r, Alu.mult), "p4ve")
        B.f(ve, VE.tensor_tensor(st_t1, st_t1, st_r, Alu.mult), "p4ve")
        mk_mm2b = vpub()
        B.w(sc, mk_mm2b)
        B.f(sc, SC.activation(st_t0, st_t1, Act.Ln, bias=EPSM_AD), "p4s")
        B.f(sc, SC.activation(st_r2, st_t0, Act.Exp, scale=-0.5), "p4s")
        mk_r2b = spub()
        B.w(ve, mk_r2b)
        B.f(ve, VE.tensor_tensor(st_rr, st_r, st_r2, Alu.mult), "p4ve")
        mk_r12 = vpub()
        B.w(ve, mk_r12)
        if c > 0:
            B.w(ve, mk_r2tm_used)
        B.f(ve, VE.scalar_tensor_tensor(r2tm[:], owb[:], st_rr, nwb[:, DM:2 * DM], Alu.mult, Alu.mult), "p4r2")
        mk_r2tm = vpub()
        mk_ow_used = mk_r2tm

        B.w(te, mk_r2tm)
        if c > 0:
            B.w(sc, mk_r2fm_used)
        for k4 in range(0, KT, 4):
            bidx = 6 + (k4 // 4) % 2
            bank = pb[bidx]
            B.guard_bank(te, bidx)
            for kk in range(4):
                k = k4 + kk
                B.i(te, TE.matmul(bank[:, 128 * kk:128 * (kk + 1)], r2tm[:, 128 * k:128 * (k + 1)], ident[:], start=True, stop=True), "p4tp")
            mk_tp = B.sems["p4tp"].mark()
            B.w(sc, mk_tp)
            mk_ev = B.i(sc, SC.copy(r2fm[:].rearrange("p (k t) -> p k t", t=128)[:, k4:k4 + 4, :], bank[:].rearrange("p (k t) -> p k t", t=128)), "p4rev")
            B.bank_guard[bidx] = mk_ev
        mk_r2tm_used = B.sems["p4tp"].mark()
        mk_r2fm = B.sems["p4rev"].mark()

        B.w(te, mk_r2fm)
        if c == 0:
            B.w(te, mk_wpt)
        for o in range(4):
            bidx = 4 + o % 2
            bank = pb[bidx]
            B.guard_bank(te, bidx)
            for k in range(KT):
                B.i(te, TE.matmul(bank[:, 0:512], r2fm[:, 128 * k:128 * (k + 1)], wpt[:, DM * k + 512 * o: DM * k + 512 * (o + 1)], start=(k == 0), stop=(k == KT - 1)), "p4pm")
            mk_pm = B.sems["p4pm"].mark()
            if o == 3:
                mk_r2fm_used = mk_pm
            B.w(ve, mk_pm)
            B.w(ve, mk_gld)
            mk_sg = B.f(ve, VE.scalar_tensor_tensor(sgb[:, 512 * o:512 * (o + 1)], bank[:, 0:512], MAGP, gb[:, 512 * o:512 * (o + 1)], Alu.mult, Alu.mult), "p4sg")
            B.bank_guard[bidx] = mk_sg
            B.w(ve, mk_x4)
            if c > 0:
                B.w(ve, mk_ow_free[o])
            mk_oc = B.f(ve, VE.tensor_tensor(outb[:, 512 * o:512 * (o + 1)], sgb[:, 512 * o:512 * (o + 1)], xb4[:, 512 * o:512 * (o + 1)], Alu.add), "p4oc")
            B.w(gp, mk_oc)
            mk_out = B.i(gp, GP.dma_start(out=y_d[CH * c:CH * (c + 1), 512 * o:512 * (o + 1)], in_=outb[:, 512 * o:512 * (o + 1)]), f"yout{o}", 16)
            if c == 0 and o == 0:
                mk_ow_free = {}
            mk_ow_free[o] = mk_out
        mk_gb_used = mk_sg
        mk_xb4_used = mk_oc

    B.barrier([gp])
    B.audit()
    return nc, ctx, None


def _consts():
    t = np.arange(128, dtype=np.float32)
    tt = np.tile(t[None, :], (128, 1))
    diff = t[None, :] - t[:, None]
    diffc = np.maximum(diff, 0.0).astype(np.float32)
    lmask = (diff >= 0).astype(np.float32)
    tmin = (128.0 - t)[:, None].astype(np.float32)
    eps = np.zeros((128, 8), np.float32)
    eps[:, 0] = 1e-5
    eps[:, 1] = np.finfo(np.float32).eps
    eps[:, 2] = 1e-8
    eps[:, 3] = 1e-6
    ident = np.eye(128, dtype=ml_dtypes.bfloat16)
    return {"c_ident": ident, "c_tt": tt.astype(np.float32), "c_diffc": diffc,
            "c_lmask": lmask, "c_tmin": tmin, "c_eps": eps}


def kernel(L0_attn_x, L0_attn_qkv_w, L0_attn_qkv_norm_w, L0_attn_proj_w,
           L0_attn_proj_norm_w, L0_attn_norm_w, L0_attn_s, L0_ffn_weight):
    x = np.ascontiguousarray(np.asarray(L0_attn_x, dtype=np.float32))
    nb = x.shape[0]
    shared = {
        "qkv_w": np.ascontiguousarray(np.asarray(L0_attn_qkv_w, np.float32)),
        "proj_w": np.ascontiguousarray(np.asarray(L0_attn_proj_w, np.float32)),
        "ffn_w": np.ascontiguousarray(np.asarray(L0_ffn_weight, np.float32)),
        "qkv_norm_w": np.tile(np.asarray(L0_attn_qkv_norm_w, np.float32).reshape(1, DM), (128, 1)),
        "proj_norm_w": np.tile(np.asarray(L0_attn_proj_norm_w, np.float32).reshape(1, DM), (128, 1)),
        "norm_w": np.tile(np.asarray(L0_attn_norm_w, np.float32).reshape(1, DM), (128, 1)),
        "s_dec": np.tile(np.asarray(L0_attn_s, np.float32).reshape(1, NH), (128, 1)),
        **_consts(),
    }
    nc, ctx, _ = build_program(nch=x.shape[1] // CH)
    with ctx:
        in_maps = [dict(shared, x=np.ascontiguousarray(x[b])) for b in range(nb)]
        res = run_bass_kernel_spmd(nc, in_maps, core_ids=list(range(nb)))
    return np.stack([np.asarray(r["y"], np.float32) for r in res.results], axis=0)
